# version 25 (final)
# baseline (speedup 1.0000x reference)
import sys
if "/opt/trn_rl_repo" not in sys.path:
    sys.path.insert(0, "/opt/trn_rl_repo")
from contextlib import ExitStack

import numpy as np
import ml_dtypes

import concourse.bass as bass
import concourse.mybir as mybir
from concourse.bass_utils import run_bass_kernel_spmd

F32 = mybir.dt.float32
BF16 = mybir.dt.bfloat16
I32 = mybir.dt.int32
AF = mybir.ActivationFunctionType
ALU = mybir.AluOpType
AX = mybir.AxisListType
BF = ml_dtypes.bfloat16

S, D, NC = 8192, 2048, 8
SC = S // NC
H, HD, WIN = 16, 128, 512
HF, R = 8192, 16
DL, DC = 512, 256
ROPE = 10000.0
NEG = -1.0e30
EPS = 1e-5
ISQH = 1.0 / np.sqrt(HD).astype(np.float32)
ISQC = 1.0 / np.sqrt(DC).astype(np.float32)

_CUT = 4
_DBG = False

_ENG = ("T", "V", "A", "P", "S")


class Sem:
    def __init__(self, es, nc, name):
        self.h = es.enter_context(nc.semaphore(name))
        self.n = 0


class _Call:
    __slots__ = ("method", "args", "kwargs", "incs")

    def __init__(self, method, args, kwargs):
        self.method, self.args, self.kwargs, self.incs = method, args, kwargs, []

    def then_inc(self, sem, k=1):
        self.incs.append((sem, k))
        sem.n += k
        return self


class _EngRec:
    def __init__(self, rec, key):
        object.__setattr__(self, "_rec", rec)
        object.__setattr__(self, "_key", key)

    def __getattr__(self, method):
        rec, key = self._rec, self._key

        def f(*args, **kwargs):
            call = _Call(method, args, kwargs)
            rec.q[key].append(call)
            return call

        return f

    def wait(self, sem, val=None):
        self._rec.q[self._key].append(
            _Call("wait_ge", (sem, sem.n if val is None else val), {}))

    def mark(self, sem, k=1):
        last = self._rec.q[self._key][-1]
        assert last.method != "wait_ge", f"mark after wait on {self._key}"
        assert not last.incs, f"stacked mark on {self._key}"
        last.then_inc(sem, k)


class Rec:
    def __init__(self):
        self.q = {e: [] for e in _ENG}
        for e in _ENG:
            setattr(self, e, _EngRec(self, e))

    def replay(self, nc):
        def runner(calls):
            def _(eng):
                for cl in calls:
                    args = [a.h if isinstance(a, Sem) else a for a in cl.args]
                    inst = getattr(eng, cl.method)(*args, **cl.kwargs)
                    for sm, k in cl.incs:
                        inst.then_inc(sm.h, k)
            return _

        with nc.Block() as blk:
            if self.q["T"]:
                blk.tensor(runner(self.q["T"]))
            if self.q["V"]:
                blk.vector(runner(self.q["V"]))
            if self.q["A"]:
                blk.scalar(runner(self.q["A"]))
            if self.q["P"]:
                blk.gpsimd(runner(self.q["P"]))
            if self.q["S"]:
                blk.sync(runner(self.q["S"]))


class C:
    pass


def _rearr_dram(t, p=128):
    return t.ap().rearrange("(dt p) n -> p dt n", p=p)


def build(cut=4):
    nc = bass.Bass("TRN2", target_bir_lowering=False,
                   detect_race_conditions=False)
    c = C()
    c.nc = nc
    es = ExitStack()
    c.es = es

    def din(name, shape, dt):
        h = nc.dram_tensor(name, list(shape), dt, kind="ExternalInput")
        setattr(c, name, h)
        return h

    din("xT_own", (D, SC), F32)
    din("xT_halo", (D, WIN), F32)
    din("wqT", (D, D), F32)
    din("wkT", (D, D), F32)
    din("wvT", (D, D), F32)
    din("woT", (D, D), F32)
    din("ta_cos", (HD, WIN + SC), F32)
    din("ta_sin", (HD, WIN + SC), F32)
    din("zadd_a_i", (128, 16), F32)
    din("ident_bf_i", (128, 128), BF16)
    din("ident_f_i", (128, 128), F32)
    din("ones_col_i", (128, 1), BF16)
    din("ones_row_i", (1, 128), F32)
    din("ones_rb_i", (1, 128), BF16)
    din("tri_prev_i", (128, 128), BF16)
    din("tri_cur_i", (128, 128), BF16)
    din("wupq_sh", (256, HF), I32)
    din("upsc_sh", (256, HF), F32)
    din("wdnq_sh", (1024, D), I32)
    din("dnsc_sh", (1024, D), F32)
    din("bup_i", (128, 64), F32)
    din("bdn_i", (128, 16), F32)
    din("loraA1", (D, R), F32)
    din("loraB1", (R, HF), F32)
    din("loraA2", (HF, R), F32)
    din("loraB2", (R, D), F32)
    din("wkvT", (D, DL), F32)
    din("wkv_nat", (DL, D), F32)
    din("wcT", (DL, DC), F32)
    din("wqcT", (DC, DC), F32)
    din("wkvcT", (DC, 2 * DC), F32)
    din("wc2lT", (DC, DL), F32)
    din("worT", (DL, D), F32)
    din("gb_i", (128, 4), F32)
    din("bor_i", (128, 16), F32)
    din("tb_cos", (DC, SC), F32)
    din("tb_sin", (DC, SC), F32)
    din("zadd_b_i", (128, 128), F32)

    c.out_xT = nc.dram_tensor("out_xT", [D, SC], F32, kind="ExternalOutput")

    _k = dict(kind="ExternalOutput") if _DBG else {}
    c.q_sp = nc.dram_tensor("q_sp", [16, 128, SC], BF16, **_k)
    c.k_sp = nc.dram_tensor("k_sp", [16, 128, WIN + SC], BF16, **_k)
    c.v_sp = nc.dram_tensor("v_sp", [12, 16, 128, 128], BF16, **_k)
    c.wv_bf = nc.dram_tensor("wv_bf", [16, 128, D], BF16, **_k)
    c.wo_bf = nc.dram_tensor("wo_bf", [16, 128, D], BF16, **_k)
    c.up_in = nc.dram_tensor("up_in", [2 * 64 * 128, 128], BF16)
    c.up_out = nc.dram_tensor("up_out", [NC * 2 * 64 * 128, 128], BF16, addr_space="Shared")
    c.dn_in = nc.dram_tensor("dn_in", [8 * 16 * 128, 128], BF16)
    c.dn_out = nc.dram_tensor("dn_out", [NC * 8 * 16 * 128, 128], BF16, addr_space="Shared")
    c.kv_in = nc.dram_tensor("kv_in", [512, 1024], BF16)
    c.k_out = nc.dram_tensor("k_out", [NC * 256, 1024], BF16, addr_space="Shared")
    c.v_out = nc.dram_tensor("v_out", [NC * 256, 1024], BF16, addr_space="Shared")

    c.sem = {n: Sem(es, nc, n) for n in
             ["one", "onep", "din", "dqo", "lp0", "lp1",
              "ld0", "ld1", "ld2", "ld3", "ld4", "ld5",
              "sp0", "sp1", "sp2", "sp3", "t", "t2", "v", "v2", "a",
              "ccu", "ccd", "cck", "cck2", "dout"]}

    def sb(name, shape, dt):
        t = es.enter_context(nc.sbuf_tensor(name, list(shape), dt))
        setattr(c, name, t)
        return t

    sb("xT", (128, 16, SC), F32)
    sb("ident_bf", (128, 128), BF16)
    sb("ident_f", (128, 128), F32)
    sb("ones_col", (128, 1), BF16)
    sb("ones_row", (1, 128), F32)
    sb("ones_rb", (1, 128), BF16)
    sb("rows_bf", (1, 512), BF16)
    sb("tri_prev", (128, 128), BF16)
    sb("tri_cur", (128, 128), BF16)
    sb("zadd_a", (128, 16), F32)
    sb("zadd_b", (128, 128), F32)
    sb("rows", (1, 1536), F32)
    sb("mcol", (128, 4), F32)
    sb("npack", (128, 4), F32)
    c.PS = [es.enter_context(nc.psum_tensor(f"ps{i}", [128, 512], F32))
            for i in range(8)]
    c.wq_bf = nc.dram_tensor("wq_bf", [16, 128, D], BF16)
    c.wk_bf = nc.dram_tensor("wk_bf", [16, 128, D], BF16)
    c.b1_bf = nc.dram_tensor("b1_bf", [R, HF], BF16)
    c.b2_bf = nc.dram_tensor("b2_bf", [R, D], BF16)

    c.cut = cut
    if cut in (10, 1, 2, 3, 4, 30, 31, 32, 33, 34, 35, 36, 37, 38, 39):
        phase1(c)
    if cut in (1, 2, 3, 4, 30, 31, 32, 33, 34, 35, 36, 37, 38, 39):
        phase2(c)
    if cut in (2, 3, 4, 30, 31, 32, 33, 34, 35, 36, 37, 38, 39):
        ffn_phase(c, 0)
    if cut in (30, 31, 32, 33, 34, 35, 3, 4, 36, 37, 38, 39):
        phase4(c, sub={30: "kv", 31: "attn", 32: "nat", 33: "nokt", 34: "nomask", 35: "noav", 36: "pre", 37: "pre2", 38: "pre3", 39: "pre4"}.get(cut, "full"))
    if cut in (4,):
        ffn_phase(c, 1)
    phase_out(c)
    es.close()
    nc.finalize()
    return c


def prep_chain(c, r, stg, chunks):
    sem = c.sem
    sin = [sem["din"], sem["lp0"]]
    sout = [sem["dqo"], sem["lp1"]]
    n = len(chunks)
    min_, mout = {}, {}

    def stg_ap(j):
        s = stg[j % 2]
        if len(chunks[j]) > 3 and chunks[j][3]:
            return s.ap().rearrange("p (a n) -> p a n", a=chunks[j][3])
        if chunks[j][2]:
            return s.ap().rearrange(chunks[j][2])
        return s[:, :, :] if len(s.shape) == 3 else s[:, :]

    def stg_in_ap(j):
        s = stg[j % 2]
        if len(chunks[j]) > 3 and chunks[j][3]:
            return s.ap().rearrange("p (a n) -> p a n", a=chunks[j][3])
        return s[:, :, :] if len(s.shape) == 3 else s[:, :]

    def emit_out(j):
        b = j % 2
        r.P.wait(sin[b], min_[j])
        r.P.dma_start(out=chunks[j][1], in_=stg_ap(j)).then_inc(sout[b], 16)
        mout[j] = (sout[b], sout[b].n)

    for j in range(n):
        b = j % 2
        if j >= 2:
            r.P.wait(*mout[j - 2])
        r.P.dma_start(out=stg_in_ap(j), in_=chunks[j][0]
                      ).then_inc(sin[b], 16)
        min_[j] = sin[b].n
        if j >= 1:
            emit_out(j - 1)
    emit_out(n - 1)
    return mout


def emit_consts(c, r):
    sem = c.sem
    for dst, srct in [(c.ident_bf, c.ident_bf_i), (c.ident_f, c.ident_f_i),
                      (c.ones_col, c.ones_col_i), (c.ones_row, c.ones_row_i),
                      (c.ones_rb, c.ones_rb_i),
                      (c.tri_prev, c.tri_prev_i), (c.tri_cur, c.tri_cur_i),
                      (c.zadd_a, c.zadd_a_i), (c.zadd_b, c.zadd_b_i)]:
        r.S.dma_start(out=dst[:, :], in_=srct[:, :]).then_inc(sem["one"], 16)


def emit_ln(c, r, src, dst, ncols, xbf, fsc, pbS, pbR, pbM, tail_wait=True,
            alt=None):
    sem = c.sem
    xflat = xbf.ap() if callable(getattr(xbf, "ap", None)) else xbf
    xbv = xflat.rearrange("p (dt n) -> p dt n", dt=16)
    nch = (ncols + 511) // 512
    pbS0, pbR0, pbM0 = pbS, pbR, pbM
    norm_marks = {}
    for ch in range(nch):
        if alt is not None and ch % 2 == 1:
            pbS, pbR, pbM = alt
        else:
            pbS, pbR, pbM = pbS0, pbR0, pbM0
        c0 = ch * 512
        n = min(512, ncols - c0)
        for dt in range(16):
            r.A.copy(xbv[:, dt, 0:n], src(dt, c0, n))
        r.A.mark(sem["a"], 1)
        r.T.wait(sem["a"])
        if alt is not None and ch >= 2:
            r.T.wait(sem["v"], norm_marks[ch - 2])
        for dt in range(16):
            r.T.matmul(pbS[0:1, 0:n], c.ones_col[:, :], xbv[:, dt, 0:n],
                       start=(dt == 0), stop=(dt == 15))
        r.T.mark(sem["t"], 1)
        r.A.wait(sem["t"])
        r.A.activation(xflat[:, 0:16 * n], xflat[:, 0:16 * n], AF.Square)
        r.A.mark(sem["a"], 1)
        r.T.wait(sem["a"])
        for dt in range(16):
            r.T.matmul(pbS[32:33, 0:n], c.ones_col[:, :], xbv[:, dt, 0:n],
                       start=(dt == 0), stop=(dt == 15))
        r.T.mark(sem["t"], 1)
        r.V.wait(sem["t"])
        r.V.tensor_scalar_mul(c.rows[0:1, 0:n], pbS[0:1, 0:n], 1.0 / D)
        r.V.tensor_scalar_mul(c.rows[0:1, 512:512 + n], pbS[32:33, 0:n], 1.0 / D)
        r.V.tensor_mul(c.rows[0:1, 1024:1024 + n], c.rows[0:1, 0:n], c.rows[0:1, 0:n])
        r.V.tensor_sub(c.rows[0:1, 512:512 + n], c.rows[0:1, 512:512 + n],
                       c.rows[0:1, 1024:1024 + n])
        r.V.tensor_scalar_add(c.rows[0:1, 512:512 + n],
                              c.rows[0:1, 512:512 + n], EPS)
        r.V.mark(sem["v"], 1)
        r.A.wait(sem["v"])
        r.A.activation(c.rows[0:1, 512:512 + n], c.rows[0:1, 512:512 + n],
                       AF.Ln)
        r.A.activation(c.rows[0:1, 512:512 + n], c.rows[0:1, 512:512 + n],
                       AF.Exp, scale=-0.5)
        r.A.mark(sem["a"], 1)
        r.V.wait(sem["a"])
        r.V.tensor_mul(c.rows[0:1, 1024:1024 + n], c.rows[0:1, 0:n],
                       c.rows[0:1, 512:512 + n])
        r.V.mark(sem["v"], 1)
        r.T.wait(sem["v"])
        r.T.matmul(pbR[:, 0:n], c.ones_row[:, :], c.rows[0:1, 512:512 + n],
                   start=True, stop=True)
        r.T.matmul(pbM[:, 0:n], c.ones_row[:, :], c.rows[0:1, 1024:1024 + n],
                   start=True, stop=True).then_inc(sem["t"], 1)
        r.V.wait(sem["t"])
        for dt in range(16):
            r.V.tensor_mul(fsc[:, 0:n], src(dt, c0, n), pbR[:, 0:n])
            r.V.tensor_tensor(dst(dt, c0, n), fsc[:, 0:n], pbM[:, 0:n],
                              op=ALU.subtract)
        r.V.mark(sem["v"], 1)
        norm_marks[ch] = sem["v"].n
        if alt is None:
            if tail_wait or ch + 1 < nch:
                r.T.wait(sem["v"])
        else:
            if ch + 2 < nch:
                pass
            if ch >= 1:
                pass
    if alt is not None and tail_wait:
        r.T.wait(sem["v"])
    return sem["v"].n


def phase1(c):
    nc, sem, PS = c.nc, c.sem, c.PS
    with ExitStack() as ph:
        lnT = ph.enter_context(nc.sbuf_tensor("lnT1", [128, 16, WIN + SC], BF16))
        tcos = ph.enter_context(nc.sbuf_tensor("tcos", [128, WIN + SC], F32))
        tsin = ph.enter_context(nc.sbuf_tensor("tsin", [128, WIN + SC], F32))

        with ExitStack() as ph2:
            r = Rec()
            xbf = ph2.enter_context(nc.sbuf_tensor("xbf1", [128, 16 * 512], BF16))
            fsc = ph2.enter_context(nc.sbuf_tensor("fsc1", [128, 512], F32))
            xTh = ph2.enter_context(nc.sbuf_tensor("xTh", [128, 16, WIN], F32))
            r.S.dma_start(out=xTh[:, :, :], in_=_rearr_dram(c.xT_halo)
                          ).then_inc(sem["one"], 16)
            halo_cnt = sem["one"].n
            emit_consts(c, r)
            fast_cnt = sem["one"].n
            r.S.dma_start(out=c.xT[:, :, :], in_=_rearr_dram(c.xT_own)
                          ).then_inc(sem["one"], 16)
            r.S.dma_start(out=tcos[:, :], in_=c.ta_cos[:, :]).then_inc(sem["one"], 16)
            r.S.dma_start(out=tsin[:, :], in_=c.ta_sin[:, :]).then_inc(sem["one"], 16)

            wqkstg = [ph2.enter_context(
                nc.sbuf_tensor(f"wqkstg{i}", [128, 16, 128], BF16))
                for i in range(2)]
            c.wqmark = prep_chain(
                c, r, wqkstg,
                [(c.wqT.ap().rearrange("(dt p) o -> p dt o", p=128)
                  [:, :, j * 128:(j + 1) * 128],
                  c.wq_bf.ap()[j, :, :], "p a n -> p (a n)")
                 for j in range(10)])

            r.V.wait(sem["one"], halo_cnt)
            r.A.wait(sem["one"], halo_cnt)
            r.T.wait(sem["one"], fast_cnt)
            emit_ln(c, r,
                    src=lambda dt, c0, n: xTh[:, dt, c0:c0 + n],
                    dst=lambda dt, c0, n: lnT[:, dt, c0:c0 + n],
                    ncols=WIN, xbf=xbf, fsc=fsc,
                    pbS=PS[6], pbR=PS[7], pbM=PS[5])
            r.V.wait(sem["one"])
            r.A.wait(sem["one"])
            emit_ln(c, r,
                    src=lambda dt, c0, n: c.xT[:, dt, c0:c0 + n],
                    dst=lambda dt, c0, n: lnT[:, dt, WIN + c0:WIN + c0 + n],
                    ncols=SC, xbf=xbf, fsc=fsc,
                    pbS=PS[2], pbR=PS[3], pbM=PS[4],
                    alt=(PS[6], PS[7], PS[5]))
            r.replay(nc)

        r = Rec()
        wst = [ph.enter_context(nc.sbuf_tensor(f"wst{i}", [128, 16, 128], BF16))
               for i in range(4)]
        qsb = ph.enter_context(nc.sbuf_tensor("qsb", [128, 1536], F32))
        tq2 = ph.enter_context(nc.sbuf_tensor("tq2", [128, 1536], F32))
        ro = [ph.enter_context(nc.sbuf_tensor(f"ro{i}", [128, 1536], BF16))
              for i in range(2)]
        vsb = [ph.enter_context(nc.sbuf_tensor(f"vsb{i}", [128, 2048], BF16))
               for i in range(2)]
        wvsl = [ph.enter_context(nc.sbuf_tensor(f"wvsl{i}", [128, 2048], BF16))
                for i in range(4)]

        wvstg = [ph.enter_context(nc.sbuf_tensor(f"wvstg{i}", [128, 2048], BF16))
                 for i in range(2)]
        wvT_t = c.wvT.ap().rearrange("(dt p) o -> p dt o", p=128)
        woT_t = c.woT.ap().rearrange("(dt p) o -> p dt o", p=128)
        wkstg = [ph.enter_context(nc.sbuf_tensor(f"wkstg{i}", [128, 16, 128],
                                                 BF16)) for i in range(2)]
        r.P.wait(sem["dqo"])
        r.P.wait(sem["lp1"])
        c.wqmark.update({10 + k: v for k, v in prep_chain(
            c, r, wkstg,
            [(c.wqT.ap().rearrange("(dt p) o -> p dt o", p=128)
              [:, :, j * 128:(j + 1) * 128],
              c.wq_bf.ap()[j, :, :], "p a n -> p (a n)")
             for j in range(10, 16)]).items()})
        r.P.wait(*c.wqmark[14])
        r.P.wait(*c.wqmark[15])
        wkmark = prep_chain(c, r, wkstg,
                            [(c.wkT.ap().rearrange("(dt p) o -> p dt o", p=128)
                              [:, :, j * 128:(j + 1) * 128],
                              c.wk_bf.ap()[j, :, :], "p a n -> p (a n)")
                             for j in range(16)])
        wvmark = prep_chain(c, r, wvstg,
                            [(wvT_t[:, j, :], c.wv_bf.ap()[j, :, :], None)
                             for j in range(16)])
        r.P.wait(sem["dqo"])
        r.P.wait(sem["lp1"])

        qkstate = dict(marks_free={}, ldmark={}, spmark={}, vmark={}, v2m={},
                       gi=0)

        def qk_sweep(wT_dram, spill, col0, ncols, nch):
            marks_free = qkstate["marks_free"]
            ldsem = [sem["ld0"], sem["ld1"]]
            spsem = [sem["sp0"], sem["sp1"]]
            ldmark = qkstate["ldmark"]
            spmark = qkstate["spmark"]
            gi0 = qkstate["gi"]

            def issue_ld(gi2):
                s4 = gi2 % 4
                wbf = c.wq_bf if gi2 < 16 else c.wk_bf
                if gi2 < 16:
                    r.A.wait(*c.wqmark[gi2])
                else:
                    r.A.wait(*wkmark[gi2 % 16])
                r.A.dma_start(
                    out=wst[s4].ap().rearrange("p a n -> p (a n)"),
                    in_=wbf.ap()[gi2 % 16, :, :],
                ).then_inc(ldsem[s4 % 2], 16)
                ldmark[gi2] = ldsem[s4 % 2].n

            for ot in range(16):
                gi = gi0 + ot
                b4 = gi % 4
                b = gi % 2
                if gi == 0:
                    for g2 in range(4):
                        issue_ld(g2)
                r.T.wait(ldsem[b4 % 2], ldmark[gi])
                if gi >= 2:
                    r.T.wait(sem["v2"], qkstate["v2m"][gi - 2])
                for dt in range(16):
                    for chn in range(nch):
                        r.T.matmul(PS[b * 3 + chn][:, 0:512],
                                   wst[b4][:, dt, :],
                                   lnT[:, dt, col0 + chn * 512:col0 + (chn + 1) * 512],
                                   start=(dt == 0), stop=(dt == 15))
                r.T.mark(sem["t2"], 1)
                marks_free[gi] = sem["t2"].n
                if gi + 4 < 32:
                    r.A.wait(sem["t2"], marks_free[gi])
                    issue_ld(gi + 4)
                r.V.wait(sem["t2"])
                if gi >= 2:
                    r.V.wait(spsem[b], spmark[gi - 2])
                n = nch * 512
                for chn in range(nch):
                    r.V.tensor_copy(qsb[:, chn * 512:(chn + 1) * 512],
                                    PS[b * 3 + chn][:, 0:512])
                r.V.mark(sem["v2"], 1)
                qkstate["v2m"][gi] = sem["v2"].n
                r.V.tensor_mul(tq2[0:64, 0:n], qsb[64:128, 0:n],
                               tsin[64:128, col0:col0 + n])
                r.V.tensor_mul(tq2[64:128, 0:n], qsb[0:64, 0:n],
                               tsin[0:64, col0:col0 + n])
                r.V.tensor_mul(qsb[:, 0:n], qsb[:, 0:n], tcos[:, col0:col0 + n])
                r.V.tensor_add(ro[b][:, 0:n], qsb[:, 0:n], tq2[:, 0:n])
                r.V.mark(sem["v"], 1)
                r.S.wait(sem["v"])
                r.S.dma_start(out=spill.ap()[ot, :, 0:n], in_=ro[b][:, 0:n]
                              ).then_inc(spsem[b], 16)
                spmark[gi] = spsem[b].n
            qkstate["gi"] = gi0 + 16

        qk_sweep(c.wqT, c.q_sp, WIN, SC, 2)
        qk_sweep(c.wkT, c.k_sp, 0, WIN + SC, 3)

        vmark = {}
        spillmark = {}
        amark_v = {}
        vldsem = [sem["ld2"], sem["ld3"]]
        vldmark = {}
        r.T.wait(sem["v2"])

        def vload(i2):
            b2 = i2 % 4
            if i2 >= 4:
                r.S.wait(sem["t2"], vmark[i2 - 4])
            if i2 < 16:
                r.S.wait(*wvmark[i2])
            r.S.dma_start(out=wvsl[b2][:, :], in_=c.wv_bf.ap()[i2 % 16, :, :]
                          ).then_inc(vldsem[b2 % 2], 16)
            vldmark[i2] = vldsem[b2 % 2].n

        def vspill(s):
            r.S.wait(sem["a"], amark_v[s])
            for st2 in range(2):
                st = s * 2 + st2
                r.S.dma_start(
                    out=c.v_sp.ap()[st, :, :, :].rearrange("ot p n -> p ot n"),
                    in_=vsb[st2][:, :].rearrange("p (ot n) -> p ot n", ot=16),
                ).then_inc(sem["sp2"], 16)
            spillmark[s] = sem["sp2"].n

        vload(0)
        vload(1)
        for i in range(96):
            stg, dt = divmod(i, 16)
            if i + 2 < 96:
                vload(i + 2)
            if dt == 2 and stg > 0:
                vspill(stg - 1)
            b = i % 4
            r.T.wait(vldsem[b % 2], vldmark[i])
            if dt == 0 and stg > 0:
                r.T.wait(sem["a"], amark_v[stg - 1])
            for st2 in range(2):
                st = stg * 2 + st2
                for chk in range(4):
                    r.T.matmul(PS[st2 * 4 + chk][:, 0:512],
                               lnT[:, dt, st * 128:(st + 1) * 128],
                               wvsl[b][:, chk * 512:(chk + 1) * 512],
                               start=(dt == 0), stop=(dt == 15))
            r.T.mark(sem["t2"], 1)
            vmark[i] = sem["t2"].n
            if dt == 15:
                r.A.wait(sem["t2"], vmark[i])
                if stg >= 2:
                    r.A.wait(sem["sp2"], spillmark[stg - 2])
                for st2 in range(2):
                    for chk in range(4):
                        r.A.copy(vsb[st2][:, chk * 512:(chk + 1) * 512],
                                 PS[st2 * 4 + chk][:, 0:512])
                r.A.mark(sem["a"], 1)
                amark_v[stg] = sem["a"].n
        vspill(5)
        r.S.wait(sem["sp0"])
        r.S.wait(sem["sp1"])
        r.S.wait(sem["sp2"])
        r.replay(nc)


def phase2(c):
    nc, sem, PS = c.nc, c.sem, c.PS
    spans = [(0, 128 * (kt + 1)) for kt in range(4)] + \
            [(128 * (kt - 4), 512 - 128 * (kt - 4)) for kt in range(4, 8)]
    r = Rec()
    with ExitStack() as ph:
        outT = ph.enter_context(nc.sbuf_tensor("outT", [128, 16, SC], BF16))
        kwin = [ph.enter_context(nc.sbuf_tensor(f"kwin{i}", [128, 1024], BF16))
                for i in range(4)]
        vbl = [ph.enter_context(nc.sbuf_tensor(f"vbl{i}", [128, 8, 128], BF16))
               for i in range(4)]
        qbl = [ph.enter_context(nc.sbuf_tensor(f"qbl{i}", [128, 512], BF16))
               for i in range(4)]
        e_sb = ph.enter_context(nc.sbuf_tensor("e_sb", [128, 16, 512], BF16))
        rb = ph.enter_context(nc.sbuf_tensor("rb", [128, 512], F32))
        wst2 = [ph.enter_context(nc.sbuf_tensor(f"wst2{i}", [128, 16, 128], BF16))
                for i in range(2)]

        dqq = [ph.enter_context(nc.sbuf_tensor(f"dqq{i}", [128, 1024], F32))
               for i in range(2)]
        dqs = [ph.enter_context(nc.sbuf_tensor(f"dqs{i}", [128, 1024], F32))
               for i in range(2)]
        dqo2 = [ph.enter_context(nc.sbuf_tensor(f"dqo2{i}", [128, 1024], BF16))
                for i in range(2)]
        up_in_t = c.up_in.ap().rearrange("(a h p) n -> a h p n", a=2, h=64, p=128)
        dn_in_t = c.dn_in.ap().rearrange("(a o p) n -> a o p n", a=8, o=16, p=128)
        wupq_t = c.wupq_sh.ap().rearrange("(a p) h -> p a h", p=128)
        upsc_t = c.upsc_sh.ap().rearrange("(a p) h -> p a h", p=128)
        wdnq_t = c.wdnq_sh.ap().rearrange("(a p) o -> p a o", p=128)
        dnsc_t = c.dnsc_sh.ap().rearrange("(a p) o -> p a o", p=128)
        dinm, vdq, outm = {}, {}, {}
        womark = {}

        dqsin = [sem["din"], sem["lp0"]]
        dqsout = [sem["dqo"], sem["lp1"]]

        def dq_in(j):
            b = j % 2
            if j < 16:
                dtl, jj = j // 8, j % 8
                r.P.dma_start(out=dqq[b][:, :],
                              in_=wupq_t[:, dtl, jj * 1024:(jj + 1) * 1024]
                              ).then_inc(dqsin[b], 16)
                r.P.dma_start(out=dqs[b][:, :],
                              in_=upsc_t[:, dtl, jj * 1024:(jj + 1) * 1024]
                              ).then_inc(dqsin[b], 16)
            else:
                a, hf = (j - 16) // 2, (j - 16) % 2
                r.P.dma_start(out=dqq[b][:, :],
                              in_=wdnq_t[:, a, hf * 1024:(hf + 1) * 1024]
                              ).then_inc(dqsin[b], 16)
                r.P.dma_start(out=dqs[b][:, :],
                              in_=dnsc_t[:, a, hf * 1024:(hf + 1) * 1024]
                              ).then_inc(dqsin[b], 16)
            dinm[j] = (dqsin[b], dqsin[b].n)

        def dq_out(j):
            b = j % 2
            r.P.wait(sem["v"], vdq[j])
            if j < 16:
                dtl, jj = j // 8, j % 8
                r.P.dma_start(
                    out=up_in_t[dtl, jj * 8:(jj + 1) * 8, :, :].rearrange(
                        "h p n -> p h n"),
                    in_=dqo2[b].ap().rearrange("p (h n) -> p h n", h=8),
                ).then_inc(dqsout[b], 16)
            else:
                a, hf = (j - 16) // 2, (j - 16) % 2
                r.P.dma_start(
                    out=dn_in_t[a, hf * 8:(hf + 1) * 8, :, :].rearrange(
                        "o p n -> p o n"),
                    in_=dqo2[b].ap().rearrange("p (o n) -> p o n", o=8),
                ).then_inc(dqsout[b], 16)
            outm[j] = (dqsout[b], dqsout[b].n)

        def emit_dq(jj):
            bq2 = jj % 2
            r.V.wait(*dinm[jj])
            if jj >= 2:
                r.V.wait(*outm[jj - 2])
            r.V.scalar_tensor_tensor(dqo2[bq2][:, :], dqq[bq2][:, :], -7.5,
                                     dqs[bq2][:, :], op0=ALU.add, op1=ALU.mult)
            r.V.mark(sem["v"], 1)
            vdq[jj] = sem["v"].n
            dq_out(jj)
            if jj + 2 < 32:
                dq_in(jj + 2)

        dq_in(0)
        dq_in(1)

        r.V.memset(e_sb[:, :, :], 0.0)
        r.V.mark(sem["v"], 1)
        r.T.wait(sem["v"])
        r.A.wait(sem["v"])

        it = [(blk, h) for blk in range(2) for h in range(16)]
        ldmark = {}
        ldsem = [sem["ld0"], sem["ld1"]]

        def stream_ld(j):
            bj = j % 4
            bl, hh = it[j]
            r.S.dma_start(out=kwin[bj][:, :],
                          in_=c.k_sp.ap()[hh, :, bl * 512:bl * 512 + 1024]
                          ).then_inc(ldsem[bj % 2], 16)
            r.S.dma_start(out=vbl[bj][:, :, :],
                          in_=c.v_sp.ap()[4 * bl:4 * bl + 8, hh, :, :]
                          .rearrange("st p n -> p st n")).then_inc(ldsem[bj % 2], 16)
            r.S.dma_start(out=qbl[bj][:, :],
                          in_=c.q_sp.ap()[hh, :, bl * 512:bl * 512 + 512]
                          ).then_inc(ldsem[bj % 2], 16)
            ldmark[j] = ldsem[bj % 2].n

        for j in (0, 1, 2, 3):
            stream_ld(j)

        STB = [0, 1, 3, 4, 5]
        sTm = {}
        expA = {}
        trim = {}
        avm = {}
        tailv = {}
        for i, (blk, h) in enumerate(it):
            b = i % 4
            es = (i % 2) * 8
            r.T.wait(ldsem[b % 2], ldmark[i])

            def sum_av(lag):
                r.T.wait(sem["v"], trim[i * 8 + lag])
                r.T.matmul(PS[6][0:1, 0:512], c.ones_col[:, :],
                           e_sb[:, es + lag, :], start=(lag == 0),
                           stop=(lag == 7))
                r.T.matmul(PS[7][:, 0:512], vbl[b][:, lag, :],
                           e_sb[:, es + lag, :], start=(lag == 0),
                           stop=(lag == 7)).then_inc(sem["t2"], 1)

            for kt in range(8):
                gi = i * 8 + kt
                qoff, n = spans[kt]
                pb = PS[STB[gi % 5]]
                if gi - 5 in expA:
                    r.T.wait(sem["a"], expA[gi - 5])
                r.T.matmul(pb[:, 0:n], kwin[b][:, 128 * kt:128 * kt + 128],
                           qbl[b][:, qoff:qoff + n], start=True, stop=True
                           ).then_inc(sem["t"], 1)
                sTm[gi] = sem["t"].n
                if i > 1 and kt == 0:
                    r.A.wait(sem["t2"], avm[i - 2])
                r.A.wait(sem["t"], sTm[gi])
                r.A.activation(e_sb[:, es + kt, qoff:qoff + n], pb[:, 0:n],
                               AF.Exp, scale=float(ISQH),
                               bias=c.zadd_a[:, blk * 8 + kt:blk * 8 + kt + 1])
                r.A.mark(sem["a"], 1)
                expA[gi] = sem["a"].n
                bq = 128 * kt if kt < 4 else qoff
                tri = c.tri_prev if kt < 4 else c.tri_cur
                r.V.wait(sem["a"], expA[gi])
                r.V.tensor_mul(e_sb[:, es + kt, bq:bq + 128],
                               e_sb[:, es + kt, bq:bq + 128],
                               tri[:, :]).then_inc(sem["v"], 1)
                trim[gi] = sem["v"].n
                if i < 16 and kt in (1, 4):
                    emit_dq(2 * i + (0 if kt == 1 else 1))
                if kt >= 4:
                    lag = kt - 4
                    if lag == 0 and i > 0:
                        r.T.wait(sem["v2"], tailv[i - 1])
                    sum_av(lag)
            for lag in (4, 5, 6, 7):
                sum_av(lag)
            avm[i] = sem["t2"].n
            if i + 4 < len(it):
                r.S.wait(sem["t2"], avm[i])
                stream_ld(i + 4)
            r.V.wait(sem["t2"], avm[i])
            r.V.tensor_copy(c.rows_bf[0:1, 0:512], PS[6][0:1, 0:512]
                            ).then_inc(sem["v2"], 1)
            r.T.wait(sem["v2"])
            r.T.matmul(PS[2][:, 0:512], c.ones_rb[:, :], c.rows_bf[0:1, 0:512],
                       start=True, stop=True).then_inc(sem["t"], 1)
            r.A.wait(sem["t"])
            r.A.activation(rb[:, :], PS[2][:, 0:512], AF.Ln)
            r.A.activation(rb[:, :], rb[:, :], AF.Exp, scale=-1.0)
            r.A.mark(sem["a"], 1)
            r.V.wait(sem["a"])
            r.V.scalar_tensor_tensor(
                outT[:, h, blk * 512:blk * 512 + 512], PS[7][:, 0:512], 1.0,
                rb[:, :], op0=ALU.mult, op1=ALU.mult)
            r.V.mark(sem["v2"], 1)
            tailv[i] = sem["v2"].n
            if i == 7:
                r.P.wait(*outm[14])
                r.P.wait(*outm[15])
                r.P.collective_compute(
                    "AllGather", ALU.bypass, replica_groups=[list(range(NC))],
                    ins=[c.up_in.ap().opt()], outs=[c.up_out.ap().opt()],
                ).then_inc(sem["ccu"], 1)
            if i == 15:
                r.P.wait(*outm[30])
                r.P.wait(*outm[31])
                r.P.collective_compute(
                    "AllGather", ALU.bypass, replica_groups=[list(range(NC))],
                    ins=[c.dn_in.ap().opt()], outs=[c.dn_out.ap().opt()],
                ).then_inc(sem["ccd"], 1)
                woT_t2 = c.woT.ap().rearrange("(dt p) o -> p dt o", p=128)
                womark.update(prep_chain(
                    c, r, dqo2,
                    [(woT_t2[:, :, j * 64:(j + 1) * 64],
                      c.wo_bf.ap()[:, :, j * 64:(j + 1) * 64].rearrange(
                          "dt p n -> p dt n"), None, 16)
                     for j in range(32)]))
        wo_ldm = {}
        wosem = [sem["ld2"], sem["ld3"]]

        def wst2_load(ot2, b2):
            r.S.wait(*womark[2 * ot2])
            r.S.wait(*womark[2 * ot2 + 1])
            r.S.dma_start(out=wst2[b2][:, :, :],
                          in_=c.wo_bf.ap()[:, :, ot2 * 128:(ot2 + 1) * 128]
                          .rearrange("dt p n -> p dt n")).then_inc(wosem[b2], 16)
            wo_ldm[ot2] = wosem[b2].n

        for ot in range(2):
            wst2_load(ot, ot)
        r.T.wait(sem["v2"])
        for ot in range(16):
            b = ot % 2
            r.T.wait(wosem[b], wo_ldm[ot])
            if ot >= 2:
                r.T.wait(sem["v"])
            for chn in range(2):
                pb = PS[b * 2 + chn]
                for hdt in range(16):
                    r.T.matmul(pb[:, 0:512], wst2[b][:, hdt, :],
                               outT[:, hdt, chn * 512:(chn + 1) * 512],
                               start=(hdt == 0), stop=(hdt == 15))
            r.T.mark(sem["t"], 1)
            if ot + 2 < 16:
                r.S.wait(sem["t"])
                wst2_load(ot + 2, b)
            r.V.wait(sem["t"])
            for chn in range(2):
                pb = PS[b * 2 + chn]
                r.V.scalar_tensor_tensor(
                    c.xT[:, ot, chn * 512:(chn + 1) * 512], pb[:, 0:512], 1.0,
                    c.xT[:, ot, chn * 512:(chn + 1) * 512],
                    op0=ALU.mult, op1=ALU.add)
            r.V.mark(sem["v"], 1)
        r.replay(nc)


def ffn_phase(c, inst):
    nc, sem, PS = c.nc, c.sem, c.PS
    up_out_t = c.up_out.ap().rearrange("(g h p) n -> g h p n", g=NC * 2, p=128)
    dn_out_t = c.dn_out.ap().rearrange("(g o p) n -> g o p n", g=NC * 8, p=128)
    r = Rec()
    with ExitStack() as ph:
        A1b = ph.enter_context(nc.sbuf_tensor(f"A1b{inst}", [128, 16, R], BF16))
        A2b = ph.enter_context(nc.sbuf_tensor(f"A2b{inst}", [128, 64, R], BF16))
        B2b = ph.enter_context(nc.sbuf_tensor(f"B2b{inst}", [R, D], BF16))
        bup = ph.enter_context(nc.sbuf_tensor(f"bup{inst}", [128, 64], F32))
        bdn = ph.enter_context(nc.sbuf_tensor(f"bdn{inst}", [128, 16], F32))
        b1st = [ph.enter_context(nc.sbuf_tensor(f"b1st{inst}{i}", [R, 512], BF16))
                for i in range(2)]
        r.P.dma_start(out=A1b[:, :, :],
                      in_=c.loraA1.ap().rearrange("(dt p) q -> p dt q", p=128)
                      ).then_inc(sem["onep"], 16)
        r.P.dma_start(out=A2b[:, :, :],
                      in_=c.loraA2.ap().rearrange("(ht p) q -> p ht q", p=128)
                      ).then_inc(sem["onep"], 16)
        r.P.dma_start(out=B2b[:, :], in_=c.loraB2[:, :]).then_inc(sem["onep"], 16)
        r.S.dma_start(out=bup[:, :], in_=c.bup_i[:, :]).then_inc(sem["one"], 16)
        r.S.dma_start(out=bdn[:, :], in_=c.bdn_i[:, :]).then_inc(sem["one"], 16)

        lnT = ph.enter_context(nc.sbuf_tensor(f"lnTf{inst}", [128, 16, SC], BF16))
        x2T = ph.enter_context(nc.sbuf_tensor(f"x2T{inst}", [128, 64, 512], BF16))
        x2flat = x2T.ap().rearrange("p a n -> p (a n)")
        xbf = x2flat[:, 16384:24576]
        fsc = ph.enter_context(nc.sbuf_tensor(f"fsc_f{inst}", [128, 512], F32))
        wup = [ph.enter_context(nc.sbuf_tensor(f"wup{inst}{i}", [128, 16, 128], BF16))
               for i in range(2)]
        wdn = [ph.enter_context(nc.sbuf_tensor(f"wdn{inst}{i}", [128, 16, 128], BF16))
               for i in range(2)]
        tT = ph.enter_context(nc.sbuf_tensor(f"tT{inst}", [16, SC], BF16))
        t2T = ph.enter_context(nc.sbuf_tensor(f"t2T{inst}", [16, 512], BF16))

        if inst == 0:
            b1tmp = x2T.ap().rearrange("p a n -> p (a n)")[0:R, 0:HF]
            r.P.dma_start(out=b1tmp, in_=c.loraB1[:, :]
                          ).then_inc(sem["din"], 16)
            r.P.wait(sem["din"])
            r.P.dma_start(out=c.b1_bf.ap(), in_=b1tmp
                          ).then_inc(sem["dqo"], 16)

        def ln_chunk(h2):
            return emit_ln(
                c, r,
                src=lambda dt, c0, n: c.xT[:, dt, h2 * 512 + c0:h2 * 512 + c0 + n],
                dst=lambda dt, c0, n: lnT[:, dt, h2 * 512 + c0:h2 * 512 + c0 + n],
                ncols=512, xbf=xbf, fsc=fsc,
                pbS=PS[7], pbR=PS[5], pbM=PS[4], tail_wait=(h2 == 0))

        def lora_up(chn):
            for dt in range(16):
                r.T.matmul(PS[6][0:16, 0:512], A1b[:, dt, :],
                           lnT[:, dt, chn * 512:(chn + 1) * 512],
                           start=(dt == 0), stop=(dt == 15))
            r.T.mark(sem["t"], 1)
            r.V.wait(sem["t"])
            r.V.tensor_copy(tT[0:16, chn * 512:(chn + 1) * 512], PS[6][0:16, 0:512]
                            ).then_inc(sem["v"], 1)
            r.T.wait(sem["v"])

        ln_chunk(0)
        r.T.wait(sem["one"])
        r.T.wait(sem["onep"])
        lora_up(0)
        ln_norm1 = ln_chunk(1)

        relu_mark = {}
        evac_mark = {}
        for half in range(2):
            s0 = half * 512
            if half == 1:
                r.T.wait(sem["v"], ln_norm1)
                lora_up(1)
            upmark = {}
            b1mark = {}
            b1used = {}
            upsem = [sem["ld0"], sem["ld1"]]
            b1sem = [sem["ld2"], sem["ld3"]]
            for pre in range(2):
                if half == 0 and pre == 0:
                    r.S.wait(sem["ccu"])
                if half == 1:
                    r.S.wait(sem["a"])
                r.S.dma_start(out=wup[pre][:, :, :],
                              in_=up_out_t[:, pre, :, :].rearrange("g p n -> p g n")
                              ).then_inc(upsem[pre], 16)
                upmark[pre] = upsem[pre].n
                if half == 0 and inst == 0 and pre == 0:
                    r.S.wait(sem["dqo"])
                r.S.dma_start(out=b1st[pre][:, :],
                              in_=c.b1_bf.ap()[:, pre * 512:(pre + 1) * 512]
                              ).then_inc(b1sem[pre], 16)
                b1mark[pre] = b1sem[pre].n
            for ht in range(64):
                b = ht % 2
                bank = ht % 4
                grp, sl = divmod(ht, 4)
                pb = PS[bank]
                r.T.wait(upsem[b], upmark[ht])
                r.T.wait(b1sem[grp % 2], b1mark[grp])
                if bank in relu_mark:
                    r.T.wait(sem["a"], relu_mark[bank])
                for dt in range(16):
                    r.T.matmul(pb[:, 0:512], wup[b][:, dt, :],
                               lnT[:, dt, s0:s0 + 512],
                               start=(dt == 0), stop=False)
                r.T.matmul(pb[:, 0:512], b1st[grp % 2][0:16, sl * 128:(sl + 1) * 128],
                           tT[0:16, s0:s0 + 512], start=False, stop=True
                           ).then_inc(sem["t"], 1)
                b1used[grp] = sem["t"].n
                if ht + 2 < 64:
                    r.S.wait(sem["t"])
                    r.S.dma_start(out=wup[b][:, :, :],
                                  in_=up_out_t[:, ht + 2, :, :].rearrange("g p n -> p g n")
                                  ).then_inc(upsem[b], 16)
                    upmark[ht + 2] = upsem[b].n
                if sl == 3 and grp + 2 < 16:
                    r.S.wait(sem["t"], b1used[grp])
                    r.S.dma_start(out=b1st[grp % 2][:, :],
                                  in_=c.b1_bf.ap()[:, (grp + 2) * 512:(grp + 3) * 512]
                                  ).then_inc(b1sem[grp % 2], 16)
                    b1mark[grp + 2] = b1sem[grp % 2].n
                r.A.wait(sem["t"])
                r.A.activation(x2T[:, ht, :], pb[:, 0:512], AF.Relu,
                               bias=bup[:, ht:ht + 1])
                r.A.mark(sem["a"], 1)
                relu_mark[bank] = sem["a"].n
            r.T.wait(sem["a"])
            for ht in range(64):
                r.T.matmul(PS[6][0:16, 0:512], A2b[:, ht, :], x2T[:, ht, :],
                           start=(ht == 0), stop=(ht == 63))
            r.T.mark(sem["t"], 1)
            r.V.wait(sem["t"])
            r.V.tensor_copy(t2T[0:16, 0:512], PS[6][0:16, 0:512]).then_inc(sem["v"], 1)
            dnmark = {}
            stage_done = {}
            dnsem = [sem["ld4"], sem["ld5"]]
            for pre in range(2):
                if half == 0 and pre == 0:
                    r.S.wait(sem["ccd"])
                if half == 1:
                    r.S.wait(sem["v"])
                r.S.dma_start(out=wdn[pre][:, :, :],
                              in_=dn_out_t[pre * 16:(pre + 1) * 16, 0, :, :]
                              .rearrange("g p n -> p g n")).then_inc(dnsem[pre], 16)
                dnmark[pre] = dnsem[pre].n
            r.T.wait(sem["v"])
            if half == 0:
                r.T.wait(sem["v"], ln_norm1)
            for ot in range(16):
                bank = 4 + ot % 2
                pb = PS[bank]
                if bank in evac_mark:
                    r.T.wait(sem["v"], evac_mark[bank])
                for hw4 in range(4):
                    stg = ot * 4 + hw4
                    r.T.wait(dnsem[hw4 % 2], dnmark[stg])
                    for hh in range(16):
                        r.T.matmul(pb[:, 0:512], wdn[hw4 % 2][:, hh, :],
                                   x2T[:, hw4 * 16 + hh, :],
                                   start=(hw4 == 0 and hh == 0), stop=False)
                    r.T.mark(sem["t2"], 1)
                    stage_done[stg] = sem["t2"].n
                    nxt = stg + 2
                    if nxt < 64:
                        n_ot, n_hw = divmod(nxt, 4)
                        r.S.wait(sem["t2"], stage_done[stg])
                        r.S.dma_start(
                            out=wdn[hw4 % 2][:, :, :],
                            in_=dn_out_t[n_hw * 16:(n_hw + 1) * 16, n_ot, :, :]
                            .rearrange("g p n -> p g n")).then_inc(dnsem[hw4 % 2], 16)
                        dnmark[nxt] = dnsem[hw4 % 2].n
                r.T.matmul(pb[:, 0:512], B2b[0:16, ot * 128:(ot + 1) * 128],
                           t2T[0:16, 0:512], start=False, stop=True
                           ).then_inc(sem["t"], 1)
                r.V.wait(sem["t"])
                r.V.scalar_tensor_tensor(
                    c.xT[:, ot, s0:s0 + 512], pb[:, 0:512],
                    bdn[:, ot:ot + 1], c.xT[:, ot, s0:s0 + 512],
                    op0=ALU.add, op1=ALU.add)
                r.V.mark(sem["v"], 1)
                evac_mark[bank] = sem["v"].n
                if inst == 1:
                    r.S.wait(sem["v"], evac_mark[bank])
                    r.S.dma_start(
                        out=c.out_xT.ap().rearrange("(dt p) n -> p dt n", p=128)
                        [:, ot, s0:s0 + 512],
                        in_=c.xT[:, ot, s0:s0 + 512]).then_inc(sem["dout"], 16)
                    c.out_streamed = True
        r.replay(nc)


def phase4(c, sub="full"):
    nc, sem, PS = c.nc, c.sem, c.PS
    r = Rec()
    k_out_t = c.k_out.ap().rearrange("(r t) n -> r t n", r=NC)
    v_out_t = c.v_out.ap().rearrange("(r t) n -> r t n", r=NC)
    with ExitStack() as ph:
        wkvb = ph.enter_context(nc.sbuf_tensor("wkvb", [128, 16, DL], BF16))
        wcb = ph.enter_context(nc.sbuf_tensor("wcb", [128, 4, DC], BF16))
        wqcb = ph.enter_context(nc.sbuf_tensor("wqcb", [128, 2, DC], BF16))
        wkvcb = ph.enter_context(nc.sbuf_tensor("wkvcb", [128, 2, 2 * DC], BF16))
        wc2lb = ph.enter_context(nc.sbuf_tensor("wc2lb", [128, 2, DL], BF16))
        gb = ph.enter_context(nc.sbuf_tensor("gbs", [128, 4], F32))
        bor = ph.enter_context(nc.sbuf_tensor("bors", [128, 16], F32))
        cosb = ph.enter_context(nc.sbuf_tensor("cosb", [128, 2, SC], F32))
        sinb = ph.enter_context(nc.sbuf_tensor("sinb", [128, 2, SC], F32))
        qTr = ph.enter_context(nc.sbuf_tensor("qTr", [128, 2, SC], BF16))
        kTr = ph.enter_context(nc.sbuf_tensor("kTr", [128, 2, SC], BF16))
        vN = ph.enter_context(nc.sbuf_tensor("vN", [128, 8, DC], BF16))
        o_sb = ph.enter_context(nc.sbuf_tensor("o_sb", [128, 2, SC], BF16))

        for dst, src, p in [
            (wkvb[:, :, :], c.wkvT.ap().rearrange("(a p) n -> p a n", p=128), 1),
            (wcb[:, :, :], c.wcT.ap().rearrange("(a p) n -> p a n", p=128), 1),
            (wqcb[:, :, :], c.wqcT.ap().rearrange("(a p) n -> p a n", p=128), 1),
            (wkvcb[:, :, :], c.wkvcT.ap().rearrange("(a p) n -> p a n", p=128), 1),
            (wc2lb[:, :, :], c.wc2lT.ap().rearrange("(a p) n -> p a n", p=128), 1),
        ]:
            r.P.dma_start(out=dst, in_=src).then_inc(sem["onep"], 16)
        r.S.dma_start(out=gb[:, :], in_=c.gb_i[:, :]).then_inc(sem["one"], 16)
        r.S.dma_start(out=bor[:, :], in_=c.bor_i[:, :]).then_inc(sem["one"], 16)
        r.S.dma_start(out=cosb[:, :, :],
                      in_=c.tb_cos.ap().rearrange("(a p) n -> p a n", p=128)
                      ).then_inc(sem["one"], 16)
        r.S.dma_start(out=sinb[:, :, :],
                      in_=c.tb_sin.ap().rearrange("(a p) n -> p a n", p=128)
                      ).then_inc(sem["one"], 16)
        r.V.wait(sem["one"])

        kvl = ph.enter_context(nc.sbuf_tensor("kvl", [128, 4, SC], BF16))
        with ExitStack() as ph2:
            lnT = ph2.enter_context(nc.sbuf_tensor("lnTb", [128, 16, SC], BF16))
            xbf = ph2.enter_context(nc.sbuf_tensor("xbf_b", [128, 16 * 512], BF16))
            fsc = ph2.enter_context(nc.sbuf_tensor("fsc_b", [128, 512], F32))
            emit_ln(c, r,
                    src=lambda dt, c0, n: c.xT[:, dt, c0:c0 + n],
                    dst=lambda dt, c0, n: lnT[:, dt, c0:c0 + n],
                    ncols=SC, xbf=xbf, fsc=fsc,
                    pbS=PS[6], pbR=PS[7], pbM=PS[5],
                    alt=(PS[2], PS[3], PS[4]))
            r.T.wait(sem["onep"])
            for lt in range(4):
                for chn in range(2):
                    pb = PS[lt % 2]
                    for dt in range(16):
                        r.T.matmul(pb[:, 0:512], wkvb[:, dt, lt * 128:(lt + 1) * 128],
                                   lnT[:, dt, chn * 512:(chn + 1) * 512],
                                   start=(dt == 0), stop=(dt == 15))
                    r.T.mark(sem["t"], 1)
                    r.V.wait(sem["t"])
                    r.V.tensor_copy(kvl[:, lt, chn * 512:(chn + 1) * 512],
                                    pb[:, 0:512]).then_inc(sem["v"], 1)
                    r.T.wait(sem["v"])
            r.replay(nc)

        r = Rec()
        ph3 = ph.enter_context(ExitStack())
        c_sb = ph3.enter_context(nc.sbuf_tensor("c_sb", [128, 2, SC], F32))
        chT = ph3.enter_context(nc.sbuf_tensor("chT", [128, 2, SC], BF16))
        vTs = ph3.enter_context(nc.sbuf_tensor("vTs", [128, 2, SC], BF16))
        qf = ph3.enter_context(nc.sbuf_tensor("qf", [128, 2, SC], F32))
        tr1 = ph3.enter_context(nc.sbuf_tensor("tr1", [128, SC], F32))
        tr2 = ph3.enter_context(nc.sbuf_tensor("tr2", [128, SC], F32))

        for dct in range(2):
            for chn in range(2):
                pb = PS[dct]
                for lt in range(4):
                    r.T.matmul(pb[:, 0:512], wcb[:, lt, dct * 128:(dct + 1) * 128],
                               kvl[:, lt, chn * 512:(chn + 1) * 512],
                               start=(lt == 0), stop=(lt == 3))
                r.T.mark(sem["t"], 1)
                r.V.wait(sem["t"])
                r.V.tensor_copy(c_sb[:, dct, chn * 512:(chn + 1) * 512], pb[:, 0:512]
                                ).then_inc(sem["v"], 1)
                r.T.wait(sem["v"])
        cbf = ph3.enter_context(nc.sbuf_tensor("cbf", [128, 2, 512], BF16))
        for chn in range(2):
            n0 = chn * 512
            for dct in range(2):
                r.V.tensor_copy(cbf[:, dct, :], c_sb[:, dct, n0:n0 + 512])
            r.V.mark(sem["v"], 1)
            r.T.wait(sem["v"])
            for dct in range(2):
                r.T.matmul(PS[6][0:1, 0:512], c.ones_col[:, :],
                           cbf[:, dct, :], start=(dct == 0), stop=(dct == 1))
            r.T.mark(sem["t"], 1)
            r.A.wait(sem["t"])
            r.A.activation(cbf[:, :, :], cbf[:, :, :], AF.Square)
            r.A.mark(sem["a"], 1)
            r.T.wait(sem["a"])
            for dct in range(2):
                r.T.matmul(PS[6][32:33, 0:512], c.ones_col[:, :],
                           cbf[:, dct, :], start=(dct == 0), stop=(dct == 1))
            r.T.mark(sem["t"], 1)
            r.V.wait(sem["t"])
            r.V.tensor_scalar_mul(c.rows[0:1, 0:512], PS[6][0:1, 0:512], 1.0 / DC)
            r.V.tensor_scalar_mul(c.rows[0:1, 512:1024], PS[6][32:33, 0:512], 1.0 / DC)
            r.V.tensor_mul(c.rows[0:1, 1024:1536], c.rows[0:1, 0:512], c.rows[0:1, 0:512])
            r.V.tensor_sub(c.rows[0:1, 512:1024], c.rows[0:1, 512:1024],
                           c.rows[0:1, 1024:1536])
            r.V.tensor_scalar_add(c.rows[0:1, 512:1024], c.rows[0:1, 512:1024], EPS)
            r.V.mark(sem["v"], 1)
            r.A.wait(sem["v"])
            r.A.activation(c.rows[0:1, 512:1024], c.rows[0:1, 512:1024], AF.Ln)
            r.A.activation(c.rows[0:1, 512:1024], c.rows[0:1, 512:1024], AF.Exp, scale=-0.5)
            r.A.mark(sem["a"], 1)
            r.V.wait(sem["a"])
            r.V.tensor_mul(c.rows[0:1, 1024:1536], c.rows[0:1, 0:512], c.rows[0:1, 512:1024])
            r.V.mark(sem["v"], 1)
            r.T.wait(sem["v"])
            r.T.matmul(PS[7][:, 0:512], c.ones_row[:, :], c.rows[0:1, 512:1024],
                       start=True, stop=True)
            r.T.matmul(PS[5][:, 0:512], c.ones_row[:, :], c.rows[0:1, 1024:1536],
                       start=True, stop=True).then_inc(sem["t"], 1)
            r.V.wait(sem["t"])
            for dct in range(2):
                r.V.tensor_mul(tr1[:, 0:512], c_sb[:, dct, n0:n0 + 512], PS[7][:, 0:512])
                r.V.tensor_tensor(tr2[:, 0:512], tr1[:, 0:512], PS[5][:, 0:512],
                                  op=ALU.subtract)
                r.V.tensor_scalar(chT[:, dct, n0:n0 + 512], tr2[:, 0:512],
                                  gb[:, dct:dct + 1], gb[:, 2 + dct:3 + dct],
                                  op0=ALU.mult, op1=ALU.add)
            r.V.mark(sem["v"], 1)
            r.T.wait(sem["v"])

        def proj2(dst_f32, wt, otile, nout_tiles):
            for t in range(nout_tiles):
                for chn in range(2):
                    pb = PS[t % 2]
                    for dct in range(2):
                        r.T.matmul(pb[:, 0:512],
                                   wt[:, dct, (otile + t) * 128:(otile + t + 1) * 128],
                                   chT[:, dct, chn * 512:(chn + 1) * 512],
                                   start=(dct == 0), stop=(dct == 1))
                    r.T.mark(sem["t"], 1)
                    r.V.wait(sem["t"])
                    r.V.tensor_copy(dst_f32[:, t, chn * 512:(chn + 1) * 512],
                                    pb[:, 0:512]).then_inc(sem["v"], 1)
                    r.T.wait(sem["v"])

        def ropeb(dst_bf, src_f32):
            for dct in range(2):
                r.V.tensor_mul(tr1[:, 0:SC], src_f32[:, dct, :], cosb[:, dct, :])
                r.V.tensor_mul(tr2[:, 0:SC], src_f32[:, 1 - dct, :], sinb[:, dct, :])
                r.V.tensor_add(dst_bf[:, dct, :], tr1[:, 0:SC], tr2[:, 0:SC])
            r.V.mark(sem["v"], 1)

        proj2(qf, wqcb, 0, 2)
        ropeb(qTr, qf)
        proj2(qf, wkvcb, 0, 2)
        ropeb(kTr, qf)
        r.S.wait(sem["v"])
        r.S.dma_start(out=c.kv_in.ap()[0:256, :].rearrange(
                          "(a p) n -> p a n", p=128),
                      in_=kTr[:, :, :]).then_inc(sem["sp0"], 16)
        r.P.wait(sem["sp0"])
        r.P.collective_compute(
            "AllGather", ALU.bypass, replica_groups=[list(range(NC))],
            ins=[c.kv_in.ap()[0:256, :].opt()], outs=[c.k_out.ap().opt()],
        ).then_inc(sem["cck"], 1)
        proj2(qf, wkvcb, 2, 2)
        for dct in range(2):
            r.V.tensor_copy(vTs[:, dct, :], qf[:, dct, :])
        r.V.mark(sem["v"], 1)
        r.T.wait(sem["v"])
        for st in range(8):
            for dct in range(2):
                r.T.transpose(PS[st % 2].ap().bitcast(BF16)[:, dct * 128:dct * 128 + 128],
                              vTs[:, dct, st * 128:(st + 1) * 128], c.ident_bf[:, :])
            r.T.mark(sem["t"], 1)
            r.V.wait(sem["t"])
            r.V.tensor_copy(vN[:, st, :], PS[st % 2].ap().bitcast(BF16)[:, 0:256]
                            ).then_inc(sem["v"], 1)
            r.T.wait(sem["v"])
        r.S.wait(sem["v"])
        for st in range(8):
            r.S.dma_start(out=c.kv_in.ap()[256 + 32 * st:256 + 32 * (st + 1), :],
                          in_=vN[:, st, :]).then_inc(sem["sp0"], 16)
        r.P.wait(sem["sp0"])
        r.P.collective_compute(
            "AllGather", ALU.bypass, replica_groups=[list(range(NC))],
            ins=[c.kv_in.ap()[256:512, :].opt()], outs=[c.v_out.ap().opt()],
        ).then_inc(sem["cck2"], 1)
        if sub == "kv":
            r.P.wait(sem["cck"])
            r.P.wait(sem["cck2"])
            r.replay(nc)
            ph3.close()
            return

        r.replay(nc)
        ph3.close()
        r = Rec()
        worb = ph.enter_context(nc.sbuf_tensor("worb", [128, 4, D], BF16))
        wkvnb = ph.enter_context(nc.sbuf_tensor("wkvnb", [128, 4, D], BF16))
        r.P.dma_start(out=worb[:, :, :],
                      in_=c.worT.ap().rearrange("(a p) n -> p a n", p=128)
                      ).then_inc(sem["onep"], 16)
        r.P.dma_start(out=wkvnb[:, :, :],
                      in_=c.wkv_nat.ap().rearrange("(a p) n -> p a n", p=128)
                      ).then_inc(sem["onep"], 16)
        kwr = [ph.enter_context(nc.sbuf_tensor(f"kwr{i}", [128, 2, SC], BF16))
               for i in range(2)]
        vwr = [ph.enter_context(nc.sbuf_tensor(f"vwr{i}", [128, 8, DC], BF16))
               for i in range(2)]
        e_b = ph.enter_context(nc.sbuf_tensor("e_b", [128, 8, 512], BF16))
        rbb = ph.enter_context(nc.sbuf_tensor("rbb", [128, 512], F32))
        zT = ph.enter_context(nc.sbuf_tensor("zTb", [128, 4, SC], BF16))

        r.S.wait(sem["cck"])
        STB4 = [0, 1, 4, 5]
        vwaited = [False]
        for g in range(2):
            q0 = g * 512
            ldm = {}
            ktsem = [sem["ld2"], sem["ld3"]]
            for pre in range(2):
                if g > 0:
                    r.S.wait(sem["t2"])
                r.S.dma_start(out=kwr[pre][:, :, :],
                              in_=k_out_t[pre, :, :]
                              .rearrange("(a p) n -> p a n", p=128)
                              ).then_inc(ktsem[pre], 16)
                if not vwaited[0]:
                    r.S.wait(sem["cck2"])
                    vwaited[0] = True
                r.S.dma_start(out=vwr[pre][:, :, :],
                              in_=v_out_t[pre, :, :].rearrange(
                                  "(st pr) (pq dc) -> (pr pq) st dc", st=8, pq=4)
                              ).then_inc(ktsem[pre], 16)
                ldm[pre] = ktsem[pre].n
            sTm = {}
            expA = {}
            avm2 = {}
            for rr in range(NC):
                r.T.wait(ktsem[rr % 2], ldm[rr])
                for ktl in range(8):
                    ktg = rr * 8 + ktl
                    pb = PS[STB4[ktg % 4]]
                    if ktg - 4 in expA:
                        r.T.wait(sem["a"], expA[ktg - 4])
                    if ktg == 0 and g > 0:
                        r.T.wait(sem["v2"])
                    for dct in range(2):
                        r.T.matmul(pb[:, 0:512],
                                   kwr[rr % 2][:, dct, ktl * 128:(ktl + 1) * 128],
                                   qTr[:, dct, q0:q0 + 512],
                                   start=(dct == 0), stop=(dct == 1))
                    r.T.mark(sem["t"], 1)
                    sTm[ktg] = sem["t"].n
                    if ktg - 8 >= 0:
                        r.A.wait(sem["t2"], avm2.get(ktg - 8, 0))
                    r.A.wait(sem["t"], sTm[ktg])
                    r.A.activation(e_b[:, ktg % 8, :], pb[:, 0:512], AF.Exp,
                                   scale=float(ISQC),
                                   bias=c.zadd_b[:, g * 64 + ktg:g * 64 + ktg + 1])
                    r.A.mark(sem["a"], 1)
                    expA[ktg] = sem["a"].n
                    if ktg >= 4:
                        lag = ktg - 4
                        r.T.wait(sem["a"], expA[lag])
                        r.T.matmul(PS[6][0:1, 0:512], c.ones_col[:, :],
                                   e_b[:, lag % 8, :], start=(lag == 0), stop=False)
                        r.T.matmul(PS[7][:, 0:512],
                                   vwr[(lag // 8) % 2][:, lag % 8, 0:128],
                                   e_b[:, lag % 8, :], start=(lag == 0), stop=False)
                        r.T.matmul(PS[3][:, 0:512],
                                   vwr[(lag // 8) % 2][:, lag % 8, 128:256],
                                   e_b[:, lag % 8, :], start=(lag == 0), stop=False
                                   ).then_inc(sem["t2"], 1)
                        avm2[lag] = sem["t2"].n
                if rr + 2 < NC:
                    r.S.wait(sem["t2"])
                    r.S.dma_start(out=kwr[rr % 2][:, :, :],
                                  in_=k_out_t[rr + 2, :, :]
                                  .rearrange("(a p) n -> p a n", p=128)
                                  ).then_inc(ktsem[rr % 2], 16)
                    r.S.dma_start(out=vwr[rr % 2][:, :, :],
                                  in_=v_out_t[rr + 2, :, :].rearrange(
                                      "(st pr) (pq dc) -> (pr pq) st dc", st=8, pq=4)
                                  ).then_inc(ktsem[rr % 2], 16)
                    ldm[rr + 2] = ktsem[rr % 2].n
            for lag in (60, 61, 62, 63):
                r.T.wait(sem["a"], expA[lag])
                r.T.matmul(PS[6][0:1, 0:512], c.ones_col[:, :],
                           e_b[:, lag % 8, :], start=False, stop=False)
                r.T.matmul(PS[7][:, 0:512], vwr[(lag // 8) % 2][:, lag % 8, 0:128],
                           e_b[:, lag % 8, :], start=False, stop=False)
                r.T.matmul(PS[3][:, 0:512], vwr[(lag // 8) % 2][:, lag % 8, 128:256],
                           e_b[:, lag % 8, :], start=False, stop=False
                           ).then_inc(sem["t2"], 1)
                avm2[lag] = sem["t2"].n
            for i2 in range(4):
                og = 64 + i2
                pb = PS[STB4[og % 4]]
                if og - 4 in expA:
                    r.T.wait(sem["a"], expA[og - 4])
                for dct in range(2):
                    r.T.matmul(pb[:, 0:512],
                               kTr[:, dct, q0 + i2 * 128:q0 + (i2 + 1) * 128],
                               qTr[:, dct, q0:q0 + 512],
                               start=(dct == 0), stop=(dct == 1))
                r.T.mark(sem["t"], 1)
                sTm[og] = sem["t"].n
                r.A.wait(sem["t2"], avm2.get(og - 8, 0))
                r.A.wait(sem["t"], sTm[og])
                r.A.activation(e_b[:, og % 8, :], pb[:, 0:512], AF.Exp,
                               scale=float(ISQC))
                r.A.mark(sem["a"], 1)
                expA[og] = sem["a"].n
                r.V.wait(sem["a"], expA[og])
                for j2 in range(4):
                    if i2 == j2:
                        r.V.tensor_mul(e_b[:, og % 8, j2 * 128:(j2 + 1) * 128],
                                       e_b[:, og % 8, j2 * 128:(j2 + 1) * 128],
                                       c.tri_cur[:, :])
                    elif i2 > j2:
                        r.V.tensor_scalar_mul(
                            e_b[:, og % 8, j2 * 128:(j2 + 1) * 128],
                            e_b[:, og % 8, j2 * 128:(j2 + 1) * 128], 0.0)
                r.V.mark(sem["v"], 1) if i2 != 0 or True else None
                vmark_own = sem["v"].n
                r.T.wait(sem["v"], vmark_own)
                last = (i2 == 3)
                r.T.matmul(PS[6][0:1, 0:512], c.ones_col[:, :], e_b[:, og % 8, :],
                           start=False, stop=last)
                r.T.matmul(PS[7][:, 0:512], vN[:, g * 4 + i2, 0:128],
                           e_b[:, og % 8, :], start=False, stop=last)
                r.T.matmul(PS[3][:, 0:512], vN[:, g * 4 + i2, 128:256],
                           e_b[:, og % 8, :], start=False, stop=last
                           ).then_inc(sem["t2"], 1)
                avm2[og] = sem["t2"].n
            r.V.wait(sem["t2"], avm2[67])
            r.V.tensor_copy(c.rows_bf[0:1, 0:512], PS[6][0:1, 0:512]
                            ).then_inc(sem["v2"], 1)
            r.T.wait(sem["v2"])
            r.T.matmul(PS[2][:, 0:512], c.ones_rb[:, :], c.rows_bf[0:1, 0:512],
                       start=True, stop=True).then_inc(sem["t"], 1)
            r.A.wait(sem["t"])
            r.A.activation(rbb[:, :], PS[2][:, 0:512], AF.Ln)
            r.A.activation(rbb[:, :], rbb[:, :], AF.Exp, scale=-1.0)
            r.A.mark(sem["a"], 1)
            r.V.wait(sem["a"])
            r.V.scalar_tensor_tensor(o_sb[:, 0, q0:q0 + 512], PS[7][:, 0:512], 1.0,
                                     rbb[:, :], op0=ALU.mult, op1=ALU.mult)
            r.V.scalar_tensor_tensor(o_sb[:, 1, q0:q0 + 512], PS[3][:, 0:512], 1.0,
                                     rbb[:, :], op0=ALU.mult, op1=ALU.mult)
            r.V.mark(sem["v2"], 1)
            r.T.wait(sem["v2"])

        for lt in range(4):
            for chn in range(2):
                pb = PS[lt % 2]
                for dct in range(2):
                    r.T.matmul(pb[:, 0:512], wc2lb[:, dct, lt * 128:(lt + 1) * 128],
                               o_sb[:, dct, chn * 512:(chn + 1) * 512],
                               start=(dct == 0), stop=(dct == 1))
                r.T.mark(sem["t"], 1)
                r.V.wait(sem["t"])
                r.V.tensor_copy(zT[:, lt, chn * 512:(chn + 1) * 512], pb[:, 0:512]
                                ).then_inc(sem["v"], 1)
                r.T.wait(sem["v"])
        r.T.wait(sem["onep"])
        for ot in range(16):
            for chn in range(2):
                pb = PS[(ot % 2) * 2 + chn]
                for lt in range(4):
                    r.T.matmul(pb[:, 0:512], wkvnb[:, lt, ot * 128:(ot + 1) * 128],
                               zT[:, lt, chn * 512:(chn + 1) * 512],
                               start=(lt == 0), stop=False)
                for lt in range(4):
                    r.T.matmul(pb[:, 0:512], worb[:, lt, ot * 128:(ot + 1) * 128],
                               zT[:, lt, chn * 512:(chn + 1) * 512],
                               start=False, stop=(lt == 3))
            r.T.mark(sem["t"], 1)
            r.V.wait(sem["t"])
            for chn in range(2):
                pb = PS[(ot % 2) * 2 + chn]
                r.V.scalar_tensor_tensor(
                    c.xT[:, ot, chn * 512:(chn + 1) * 512], pb[:, 0:512],
                    bor[:, ot:ot + 1], c.xT[:, ot, chn * 512:(chn + 1) * 512],
                    op0=ALU.add, op1=ALU.add)
            r.V.mark(sem["v"], 1)
            r.T.wait(sem["v"])
        r.replay(nc)


def phase_out(c):
    nc, sem = c.nc, c.sem
    r = Rec()
    if getattr(c, "out_streamed", False):
        r.S.wait(sem["dout"])
    else:
        r.S.dma_start(out=c.out_xT.ap().rearrange("(dt p) n -> p dt n", p=128),
                      in_=c.xT[:, :, :]).then_inc(sem["dout"], 16)
        r.S.wait(sem["dout"])
    r.replay(nc)


def _host_prep(I):
    f4 = np.float32
    x = np.ascontiguousarray(I["L0_attn_a_x"][0], dtype=f4)
    perm = np.concatenate([np.arange(0, HD, 2), np.arange(1, HD, 2)])
    Wq, Wk = I["L0_attn_a_Wq"], I["L0_attn_a_Wk"]
    wqT = np.ascontiguousarray(Wq.reshape(H, HD, D)[:, perm, :].reshape(D, D).T, f4)
    wkT = np.ascontiguousarray(Wk.reshape(H, HD, D)[:, perm, :].reshape(D, D).T, f4)
    wvT = np.ascontiguousarray(I["L0_attn_a_Wv"].T, f4)
    woT = np.ascontiguousarray(I["L0_attn_a_Wo"].T, f4)
    inv = (1.0 / ROPE ** (np.arange(0, HD, 2) / HD)).astype(np.float64)

    ident_bf = np.eye(128, dtype=BF)
    ident_f = np.eye(128, dtype=f4)
    ones_col = np.ones((128, 1), dtype=BF)
    ones_row = np.ones((1, 128), dtype=f4)
    p_ = np.arange(128)[:, None]
    i_ = np.arange(128)[None, :]
    tri_prev = (p_ > i_).astype(BF)
    tri_cur = (p_ <= i_).astype(BF)

    wupqT = np.ascontiguousarray(I["L0_ffn_w_up_q"].T.astype(np.int32))
    upscT = np.ascontiguousarray(np.repeat(I["L0_ffn_w_up_scale"], 64, axis=1).T.astype(f4))
    wdnqT = np.ascontiguousarray(I["L0_ffn_w_down_q"].T.astype(np.int32))
    dnscT = np.ascontiguousarray(np.repeat(I["L0_ffn_w_down_scale"], 64, axis=1).T.astype(f4))
    bup = np.ascontiguousarray(I["L0_ffn_b_up"].astype(f4).reshape(64, 128).T)
    bdn = np.ascontiguousarray(I["L0_ffn_b_down"].astype(f4).reshape(16, 128).T)

    wkvT = np.ascontiguousarray(I["L0_attn_b_W_to_kv"].T, f4)
    wkv_nat = np.ascontiguousarray(I["L0_attn_b_W_to_kv"], f4)
    wcT = np.ascontiguousarray(I["L0_attn_b_W_compress"].T, f4)
    wqcT = np.ascontiguousarray(I["L0_attn_b_W_q_comp"].T, f4)
    wkvcT = np.ascontiguousarray(I["L0_attn_b_W_kv_comp"].T, f4)
    wc2lT = np.ascontiguousarray(I["L0_attn_b_W_comp2lat"].T, f4)
    worT = np.ascontiguousarray(I["L0_attn_b_W_out_res"].T, f4)
    gb = np.zeros((128, 4), f4)
    gb[:, 0] = I["L0_attn_b_ln_gamma"][0:128]
    gb[:, 1] = I["L0_attn_b_ln_gamma"][128:256]
    gb[:, 2] = I["L0_attn_b_ln_beta"][0:128]
    gb[:, 3] = I["L0_attn_b_ln_beta"][128:256]
    bor = np.ascontiguousarray(I["L0_attn_b_b_out_res"].astype(f4).reshape(16, 128).T)
    invb = (1.0 / ROPE ** (np.arange(0, DC, 2) / DC)).astype(np.float64)

    in_maps = []
    for cc in range(NC):
        r0 = cc * SC
        xo = np.ascontiguousarray(x[r0:r0 + SC].T)
        xh = (np.zeros((D, WIN), f4) if cc == 0
              else np.ascontiguousarray(x[r0 - WIN:r0].T))
        pos = np.arange(r0 - WIN, r0 + SC, dtype=np.float64)
        th = pos[:, None] * inv[None, :]
        ta_cos = np.ascontiguousarray(
            np.concatenate([np.cos(th), np.cos(th)], 1).T.astype(f4))
        ta_sin = np.ascontiguousarray(
            np.concatenate([np.sin(th), -np.sin(th)], 1).T.astype(f4))
        za = np.zeros((128, 16), f4)
        if cc == 0:
            za[:, 0:4] = NEG
        posb = np.arange(r0, r0 + SC, dtype=np.float64)
        frb = posb[:, None] * invb[None, :]
        tb_cos = np.ascontiguousarray(
            np.concatenate([np.cos(frb), np.cos(frb)], 1).T.astype(f4))
        tb_sin = np.ascontiguousarray(
            np.concatenate([-np.sin(frb), np.sin(frb)], 1).T.astype(f4))
        zb = np.zeros((128, 128), f4)
        for g in range(2):
            d0 = 8 * cc + 4 * g
            zb[:, g * 64 + d0: g * 64 + 64] = NEG
        m = dict(
            xT_own=xo, xT_halo=xh,
            wqT=wqT, wkT=wkT, wvT=wvT, woT=woT,
            ta_cos=ta_cos, ta_sin=ta_sin, zadd_a_i=za,
            ident_bf_i=ident_bf, ident_f_i=ident_f,
            ones_col_i=ones_col, ones_row_i=ones_row,
            ones_rb_i=np.ones((1, 128), dtype=ml_dtypes.bfloat16),
            tri_prev_i=tri_prev, tri_cur_i=tri_cur,
            wupq_sh=np.ascontiguousarray(wupqT[cc * 256:(cc + 1) * 256]),
            upsc_sh=np.ascontiguousarray(upscT[cc * 256:(cc + 1) * 256]),
            wdnq_sh=np.ascontiguousarray(wdnqT[cc * 1024:(cc + 1) * 1024]),
            dnsc_sh=np.ascontiguousarray(dnscT[cc * 1024:(cc + 1) * 1024]),
            bup_i=bup, bdn_i=bdn,
            loraA1=np.ascontiguousarray(I["L0_ffn_w_up_lora_a"], f4),
            loraB1=np.ascontiguousarray(I["L0_ffn_w_up_lora_b"], f4),
            loraA2=np.ascontiguousarray(I["L0_ffn_w_down_lora_a"], f4),
            loraB2=np.ascontiguousarray(I["L0_ffn_w_down_lora_b"], f4),
            wkvT=wkvT, wkv_nat=wkv_nat, wcT=wcT, wqcT=wqcT, wkvcT=wkvcT,
            wc2lT=wc2lT, worT=worT, gb_i=gb, bor_i=bor,
            tb_cos=tb_cos, tb_sin=tb_sin, zadd_b_i=zb,
        )
        in_maps.append(m)
    return in_maps


def kernel(**inputs):
    in_maps = _host_prep(inputs)
    c = build(_CUT)
    res = run_bass_kernel_spmd(c.nc, in_maps, core_ids=list(range(NC)))
    shards = [res.results[i]["out_xT"] for i in range(NC)]
    out = np.concatenate([s.T for s in shards], axis=0)[None]
    return np.ascontiguousarray(out, dtype=np.float32)


if __name__ == "__main__":
    print("kernel module (see validate.py)")


# version 27
# speedup vs baseline: 1.0587x; 1.0587x over previous
import sys
if "/opt/trn_rl_repo" not in sys.path:
    sys.path.insert(0, "/opt/trn_rl_repo")
from contextlib import ExitStack

import numpy as np
import ml_dtypes

import concourse.bass as bass
import concourse.mybir as mybir
from concourse.bass_utils import run_bass_kernel_spmd

F32 = mybir.dt.float32
BF16 = mybir.dt.bfloat16
I32 = mybir.dt.int32
AF = mybir.ActivationFunctionType
ALU = mybir.AluOpType
AX = mybir.AxisListType
BF = ml_dtypes.bfloat16

S, D, NC = 8192, 2048, 8
SC = S // NC
H, HD, WIN = 16, 128, 512
HF, R = 8192, 16
DL, DC = 512, 256
ROPE = 10000.0
NEG = -1.0e30
EPS = 1e-5
ISQH = 1.0 / np.sqrt(HD).astype(np.float32)
ISQC = 1.0 / np.sqrt(DC).astype(np.float32)

_CUT = 4
_DBG = False

_ENG = ("T", "V", "A", "P", "S")


class Sem:
    def __init__(self, es, nc, name):
        self.h = es.enter_context(nc.semaphore(name))
        self.n = 0


class _Call:
    __slots__ = ("method", "args", "kwargs", "incs")

    def __init__(self, method, args, kwargs):
        self.method, self.args, self.kwargs, self.incs = method, args, kwargs, []

    def then_inc(self, sem, k=1):
        self.incs.append((sem, k))
        sem.n += k
        return self


class _EngRec:
    def __init__(self, rec, key):
        object.__setattr__(self, "_rec", rec)
        object.__setattr__(self, "_key", key)

    def __getattr__(self, method):
        rec, key = self._rec, self._key

        def f(*args, **kwargs):
            call = _Call(method, args, kwargs)
            rec.q[key].append(call)
            return call

        return f

    def wait(self, sem, val=None):
        self._rec.q[self._key].append(
            _Call("wait_ge", (sem, sem.n if val is None else val), {}))

    def mark(self, sem, k=1):
        last = self._rec.q[self._key][-1]
        assert last.method != "wait_ge", f"mark after wait on {self._key}"
        assert not last.incs, f"stacked mark on {self._key}"
        last.then_inc(sem, k)


class Rec:
    def __init__(self):
        self.q = {e: [] for e in _ENG}
        for e in _ENG:
            setattr(self, e, _EngRec(self, e))

    def replay(self, nc):
        def runner(calls):
            def _(eng):
                for cl in calls:
                    args = [a.h if isinstance(a, Sem) else a for a in cl.args]
                    inst = getattr(eng, cl.method)(*args, **cl.kwargs)
                    for sm, k in cl.incs:
                        inst.then_inc(sm.h, k)
            return _

        with nc.Block() as blk:
            if self.q["T"]:
                blk.tensor(runner(self.q["T"]))
            if self.q["V"]:
                blk.vector(runner(self.q["V"]))
            if self.q["A"]:
                blk.scalar(runner(self.q["A"]))
            if self.q["P"]:
                blk.gpsimd(runner(self.q["P"]))
            if self.q["S"]:
                blk.sync(runner(self.q["S"]))


class C:
    pass


def _rearr_dram(t, p=128):
    return t.ap().rearrange("(dt p) n -> p dt n", p=p)


def build(cut=4):
    nc = bass.Bass("TRN2", target_bir_lowering=False,
                   detect_race_conditions=False)
    c = C()
    c.nc = nc
    es = ExitStack()
    c.es = es

    def din(name, shape, dt):
        h = nc.dram_tensor(name, list(shape), dt, kind="ExternalInput")
        setattr(c, name, h)
        return h

    din("xT_own", (D, SC), F32)
    din("xT_halo", (D, WIN), F32)
    din("wqT", (D, D), F32)
    din("wkT", (D, D), F32)
    din("wvT", (D, D), F32)
    din("woT", (D, D), F32)
    din("ta_cos", (HD, WIN + SC), F32)
    din("ta_sin", (HD, WIN + SC), F32)
    din("zadd_a_i", (128, 16), F32)
    din("ident_bf_i", (128, 128), BF16)
    din("ident_f_i", (128, 128), F32)
    din("ones_col_i", (128, 1), BF16)
    din("ones_row_i", (1, 128), F32)
    din("ones_rb_i", (1, 128), BF16)
    din("tri_prev_i", (128, 128), BF16)
    din("tri_cur_i", (128, 128), BF16)
    din("wupq_sh", (256, HF), I32)
    din("upsc_sh", (256, HF), F32)
    din("wdnq_sh", (1024, D), I32)
    din("dnsc_sh", (1024, D), F32)
    din("bup_i", (128, 64), F32)
    din("bdn_i", (128, 16), F32)
    din("loraA1", (D, R), F32)
    din("loraB1", (R, HF), F32)
    din("loraA2", (HF, R), F32)
    din("loraB2", (R, D), F32)
    din("wkvT", (D, DL), F32)
    din("wkv_nat", (DL, D), F32)
    din("wcT", (DL, DC), F32)
    din("wqcT", (DC, DC), F32)
    din("wkvcT", (DC, 2 * DC), F32)
    din("wc2lT", (DC, DL), F32)
    din("worT", (DL, D), F32)
    din("gb_i", (128, 4), F32)
    din("bor_i", (128, 16), F32)
    din("tb_cos", (DC, SC), F32)
    din("tb_sin", (DC, SC), F32)
    din("zadd_b_i", (128, 128), F32)

    c.out_xT = nc.dram_tensor("out_xT", [D, SC], F32, kind="ExternalOutput")

    _k = dict(kind="ExternalOutput") if _DBG else {}
    c.q_sp = nc.dram_tensor("q_sp", [16, 128, SC], BF16, **_k)
    c.k_sp = nc.dram_tensor("k_sp", [16, 128, WIN + SC], BF16, **_k)
    c.v_sp = nc.dram_tensor("v_sp", [12, 16, 128, 128], BF16, **_k)
    c.wv_bf = nc.dram_tensor("wv_bf", [16, 128, D], BF16, **_k)
    c.wo_bf = nc.dram_tensor("wo_bf", [16, 128, D], BF16, **_k)
    c.up_in = nc.dram_tensor("up_in", [2 * 64 * 128, 128], BF16)
    c.up_out = nc.dram_tensor("up_out", [NC * 2 * 64 * 128, 128], BF16, addr_space="Shared")
    c.dn_in = nc.dram_tensor("dn_in", [8 * 16 * 128, 128], BF16)
    c.dn_out = nc.dram_tensor("dn_out", [NC * 8 * 16 * 128, 128], BF16, addr_space="Shared")
    c.kv_in = nc.dram_tensor("kv_in", [512, 1024], BF16)
    c.k_out = nc.dram_tensor("k_out", [NC * 256, 1024], BF16, addr_space="Shared")
    c.v_out = nc.dram_tensor("v_out", [NC * 256, 1024], BF16, addr_space="Shared")

    c.sem = {n: Sem(es, nc, n) for n in
             ["one", "onep", "din", "dqo", "lp0", "lp1",
              "ld0", "ld1", "ld2", "ld3", "ld4", "ld5",
              "sp0", "sp1", "sp2", "sp3", "t", "t2", "v", "v2", "a",
              "ccu", "ccd", "cck", "cck2", "dout"]}

    def sb(name, shape, dt):
        t = es.enter_context(nc.sbuf_tensor(name, list(shape), dt))
        setattr(c, name, t)
        return t

    sb("xT", (128, 16, SC), F32)
    sb("ident_bf", (128, 128), BF16)
    sb("ident_f", (128, 128), F32)
    sb("ones_col", (128, 1), BF16)
    sb("ones_row", (1, 128), F32)
    sb("ones_rb", (1, 128), BF16)
    sb("rows_bf", (1, 512), BF16)
    sb("tri_prev", (128, 128), BF16)
    sb("tri_cur", (128, 128), BF16)
    sb("zadd_a", (128, 16), F32)
    sb("zadd_b", (128, 128), F32)
    sb("rows", (1, 1536), F32)
    sb("mcol", (128, 4), F32)
    sb("npack", (128, 4), F32)
    c.PS = [es.enter_context(nc.psum_tensor(f"ps{i}", [128, 512], F32))
            for i in range(8)]
    c.wq_bf = nc.dram_tensor("wq_bf", [16, 128, D], BF16)
    c.wk_bf = nc.dram_tensor("wk_bf", [16, 128, D], BF16)
    c.b1_bf = nc.dram_tensor("b1_bf", [R, HF], BF16)
    c.b2_bf = nc.dram_tensor("b2_bf", [R, D], BF16)

    c.cut = cut
    if cut in (10, 1, 2, 3, 4, 30, 31, 32, 33, 34, 35, 36, 37, 38, 39):
        phase1(c)
    if cut in (1, 2, 3, 4, 30, 31, 32, 33, 34, 35, 36, 37, 38, 39):
        phase2(c)
    if cut in (2, 3, 4, 30, 31, 32, 33, 34, 35, 36, 37, 38, 39):
        ffn_phase(c, 0)
    if cut in (30, 31, 32, 33, 34, 35, 3, 4, 36, 37, 38, 39):
        phase4(c, sub={30: "kv", 31: "attn", 32: "nat", 33: "nokt", 34: "nomask", 35: "noav", 36: "pre", 37: "pre2", 38: "pre3", 39: "pre4"}.get(cut, "full"))
    if cut in (4,):
        ffn_phase(c, 1)
    phase_out(c)
    es.close()
    nc.finalize()
    return c


def prep_chain(c, r, stg, chunks):
    sem = c.sem
    sin = [sem["din"], sem["lp0"]]
    sout = [sem["dqo"], sem["lp1"]]
    n = len(chunks)
    min_, mout = {}, {}

    def stg_ap(j):
        s = stg[j % 2]
        if len(chunks[j]) > 3 and chunks[j][3]:
            return s.ap().rearrange("p (a n) -> p a n", a=chunks[j][3])
        if chunks[j][2]:
            return s.ap().rearrange(chunks[j][2])
        return s[:, :, :] if len(s.shape) == 3 else s[:, :]

    def stg_in_ap(j):
        s = stg[j % 2]
        if len(chunks[j]) > 3 and chunks[j][3]:
            return s.ap().rearrange("p (a n) -> p a n", a=chunks[j][3])
        return s[:, :, :] if len(s.shape) == 3 else s[:, :]

    def emit_out(j):
        b = j % 2
        r.P.wait(sin[b], min_[j])
        r.P.dma_start(out=chunks[j][1], in_=stg_ap(j)).then_inc(sout[b], 16)
        mout[j] = (sout[b], sout[b].n)

    for j in range(n):
        b = j % 2
        if j >= 2:
            r.P.wait(*mout[j - 2])
        r.P.dma_start(out=stg_in_ap(j), in_=chunks[j][0]
                      ).then_inc(sin[b], 16)
        min_[j] = sin[b].n
        if j >= 1:
            emit_out(j - 1)
    emit_out(n - 1)
    return mout


def emit_consts(c, r):
    sem = c.sem
    for dst, srct in [(c.ident_bf, c.ident_bf_i), (c.ident_f, c.ident_f_i),
                      (c.ones_col, c.ones_col_i), (c.ones_row, c.ones_row_i),
                      (c.ones_rb, c.ones_rb_i),
                      (c.tri_prev, c.tri_prev_i), (c.tri_cur, c.tri_cur_i),
                      (c.zadd_a, c.zadd_a_i), (c.zadd_b, c.zadd_b_i)]:
        r.S.dma_start(out=dst[:, :], in_=srct[:, :]).then_inc(sem["one"], 16)


def emit_ln(c, r, src, dst, ncols, xbf, fsc, pbS, pbR, pbM, tail_wait=True,
            alt=None):
    sem = c.sem
    xflat = xbf.ap() if callable(getattr(xbf, "ap", None)) else xbf
    xbv = xflat.rearrange("p (dt n) -> p dt n", dt=16)
    nch = (ncols + 511) // 512
    pbS0, pbR0, pbM0 = pbS, pbR, pbM
    norm_marks = {}
    for ch in range(nch):
        if alt is not None and ch % 2 == 1:
            pbS, pbR, pbM = alt
        else:
            pbS, pbR, pbM = pbS0, pbR0, pbM0
        c0 = ch * 512
        n = min(512, ncols - c0)
        for dt in range(16):
            r.A.copy(xbv[:, dt, 0:n], src(dt, c0, n))
        r.A.mark(sem["a"], 1)
        r.T.wait(sem["a"])
        if alt is not None and ch >= 2:
            r.T.wait(sem["v"], norm_marks[ch - 2])
        for dt in range(16):
            r.T.matmul(pbS[0:1, 0:n], c.ones_col[:, :], xbv[:, dt, 0:n],
                       start=(dt == 0), stop=(dt == 15))
        r.T.mark(sem["t"], 1)
        r.A.wait(sem["t"])
        r.A.activation(xflat[:, 0:16 * n], xflat[:, 0:16 * n], AF.Square)
        r.A.mark(sem["a"], 1)
        r.T.wait(sem["a"])
        for dt in range(16):
            r.T.matmul(pbS[32:33, 0:n], c.ones_col[:, :], xbv[:, dt, 0:n],
                       start=(dt == 0), stop=(dt == 15))
        r.T.mark(sem["t"], 1)
        r.V.wait(sem["t"])
        r.V.tensor_scalar_mul(c.rows[0:1, 0:n], pbS[0:1, 0:n], 1.0 / D)
        r.V.tensor_scalar_mul(c.rows[0:1, 512:512 + n], pbS[32:33, 0:n], 1.0 / D)
        r.V.tensor_mul(c.rows[0:1, 1024:1024 + n], c.rows[0:1, 0:n], c.rows[0:1, 0:n])
        r.V.tensor_sub(c.rows[0:1, 512:512 + n], c.rows[0:1, 512:512 + n],
                       c.rows[0:1, 1024:1024 + n])
        r.V.tensor_scalar_add(c.rows[0:1, 512:512 + n],
                              c.rows[0:1, 512:512 + n], EPS)
        r.V.mark(sem["v"], 1)
        r.A.wait(sem["v"])
        r.A.activation(c.rows[0:1, 512:512 + n], c.rows[0:1, 512:512 + n],
                       AF.Ln)
        r.A.activation(c.rows[0:1, 512:512 + n], c.rows[0:1, 512:512 + n],
                       AF.Exp, scale=-0.5)
        r.A.mark(sem["a"], 1)
        r.V.wait(sem["a"])
        r.V.tensor_mul(c.rows[0:1, 1024:1024 + n], c.rows[0:1, 0:n],
                       c.rows[0:1, 512:512 + n])
        r.V.mark(sem["v"], 1)
        r.T.wait(sem["v"])
        r.T.matmul(pbR[:, 0:n], c.ones_row[:, :], c.rows[0:1, 512:512 + n],
                   start=True, stop=True)
        r.T.matmul(pbM[:, 0:n], c.ones_row[:, :], c.rows[0:1, 1024:1024 + n],
                   start=True, stop=True).then_inc(sem["t"], 1)
        r.V.wait(sem["t"])
        for dt in range(16):
            r.V.tensor_mul(fsc[:, 0:n], src(dt, c0, n), pbR[:, 0:n])
            r.V.tensor_tensor(dst(dt, c0, n), fsc[:, 0:n], pbM[:, 0:n],
                              op=ALU.subtract)
        r.V.mark(sem["v"], 1)
        norm_marks[ch] = sem["v"].n
        if alt is None:
            if tail_wait or ch + 1 < nch:
                r.T.wait(sem["v"])
        else:
            if ch + 2 < nch:
                pass
            if ch >= 1:
                pass
    if alt is not None and tail_wait:
        r.T.wait(sem["v"])
    return sem["v"].n


def phase1(c):
    nc, sem, PS = c.nc, c.sem, c.PS
    with ExitStack() as ph:
        lnT = ph.enter_context(nc.sbuf_tensor("lnT1", [128, 16, WIN + SC], BF16))
        tcos = ph.enter_context(nc.sbuf_tensor("tcos", [128, WIN + SC], F32))
        tsin = ph.enter_context(nc.sbuf_tensor("tsin", [128, WIN + SC], F32))

        with ExitStack() as ph2:
            r = Rec()
            xbf = ph2.enter_context(nc.sbuf_tensor("xbf1", [128, 16 * 512], BF16))
            fsc = ph2.enter_context(nc.sbuf_tensor("fsc1", [128, 512], F32))
            xTh = ph2.enter_context(nc.sbuf_tensor("xTh", [128, 16, WIN], F32))
            r.S.dma_start(out=xTh[:, :, :], in_=_rearr_dram(c.xT_halo)
                          ).then_inc(sem["one"], 16)
            halo_cnt = sem["one"].n
            emit_consts(c, r)
            fast_cnt = sem["one"].n
            r.S.dma_start(out=c.xT[:, :, :], in_=_rearr_dram(c.xT_own)
                          ).then_inc(sem["one"], 16)
            r.S.dma_start(out=tcos[:, :], in_=c.ta_cos[:, :]).then_inc(sem["one"], 16)
            r.S.dma_start(out=tsin[:, :], in_=c.ta_sin[:, :]).then_inc(sem["one"], 16)

            wqkstg = [ph2.enter_context(
                nc.sbuf_tensor(f"wqkstg{i}", [128, 16, 128], BF16))
                for i in range(2)]
            c.wqmark = prep_chain(
                c, r, wqkstg,
                [(c.wqT.ap().rearrange("(dt p) o -> p dt o", p=128)
                  [:, :, j * 128:(j + 1) * 128],
                  c.wq_bf.ap()[j, :, :], "p a n -> p (a n)")
                 for j in range(10)])

            r.V.wait(sem["one"], halo_cnt)
            r.A.wait(sem["one"], halo_cnt)
            r.T.wait(sem["one"], fast_cnt)
            emit_ln(c, r,
                    src=lambda dt, c0, n: xTh[:, dt, c0:c0 + n],
                    dst=lambda dt, c0, n: lnT[:, dt, c0:c0 + n],
                    ncols=WIN, xbf=xbf, fsc=fsc,
                    pbS=PS[6], pbR=PS[7], pbM=PS[5])
            r.V.wait(sem["one"])
            r.A.wait(sem["one"])
            emit_ln(c, r,
                    src=lambda dt, c0, n: c.xT[:, dt, c0:c0 + n],
                    dst=lambda dt, c0, n: lnT[:, dt, WIN + c0:WIN + c0 + n],
                    ncols=SC, xbf=xbf, fsc=fsc,
                    pbS=PS[2], pbR=PS[3], pbM=PS[4],
                    alt=(PS[6], PS[7], PS[5]))
            r.replay(nc)

        r = Rec()
        wst = [ph.enter_context(nc.sbuf_tensor(f"wst{i}", [128, 16, 128], BF16))
               for i in range(4)]
        qsb = ph.enter_context(nc.sbuf_tensor("qsb", [128, 1536], F32))
        tq2 = ph.enter_context(nc.sbuf_tensor("tq2", [128, 1536], F32))
        ro = [ph.enter_context(nc.sbuf_tensor(f"ro{i}", [128, 1536], BF16))
              for i in range(2)]
        vsb = [ph.enter_context(nc.sbuf_tensor(f"vsb{i}", [128, 2048], BF16))
               for i in range(2)]
        wvsl = [ph.enter_context(nc.sbuf_tensor(f"wvsl{i}", [128, 2048], BF16))
                for i in range(4)]

        wvstg = [ph.enter_context(nc.sbuf_tensor(f"wvstg{i}", [128, 2048], BF16))
                 for i in range(2)]
        wvT_t = c.wvT.ap().rearrange("(dt p) o -> p dt o", p=128)
        woT_t = c.woT.ap().rearrange("(dt p) o -> p dt o", p=128)
        wkstg = [ph.enter_context(nc.sbuf_tensor(f"wkstg{i}", [128, 16, 128],
                                                 BF16)) for i in range(2)]
        r.P.wait(sem["dqo"])
        r.P.wait(sem["lp1"])
        c.wqmark.update({10 + k: v for k, v in prep_chain(
            c, r, wkstg,
            [(c.wqT.ap().rearrange("(dt p) o -> p dt o", p=128)
              [:, :, j * 128:(j + 1) * 128],
              c.wq_bf.ap()[j, :, :], "p a n -> p (a n)")
             for j in range(10, 16)]).items()})
        r.P.wait(*c.wqmark[14])
        r.P.wait(*c.wqmark[15])
        wkmark = prep_chain(c, r, wkstg,
                            [(c.wkT.ap().rearrange("(dt p) o -> p dt o", p=128)
                              [:, :, j * 128:(j + 1) * 128],
                              c.wk_bf.ap()[j, :, :], "p a n -> p (a n)")
                             for j in range(16)])
        wvmark = prep_chain(c, r, wvstg,
                            [(wvT_t[:, j, :], c.wv_bf.ap()[j, :, :], None)
                             for j in range(16)])
        r.P.wait(sem["dqo"])
        r.P.wait(sem["lp1"])

        qkstate = dict(marks_free={}, ldmark={}, spmark={}, vmark={}, v2m={},
                       gi=0)

        def qk_sweep(wT_dram, spill, col0, ncols, nch):
            marks_free = qkstate["marks_free"]
            ldsem = [sem["ld0"], sem["ld1"]]
            spsem = [sem["sp0"], sem["sp1"]]
            ldmark = qkstate["ldmark"]
            spmark = qkstate["spmark"]
            gi0 = qkstate["gi"]

            def issue_ld(gi2):
                s4 = gi2 % 4
                wbf = c.wq_bf if gi2 < 16 else c.wk_bf
                if gi2 < 16:
                    r.A.wait(*c.wqmark[gi2])
                else:
                    r.A.wait(*wkmark[gi2 % 16])
                r.A.dma_start(
                    out=wst[s4].ap().rearrange("p a n -> p (a n)"),
                    in_=wbf.ap()[gi2 % 16, :, :],
                ).then_inc(ldsem[s4 % 2], 16)
                ldmark[gi2] = ldsem[s4 % 2].n

            for ot in range(16):
                gi = gi0 + ot
                b4 = gi % 4
                b = gi % 2
                if gi == 0:
                    for g2 in range(4):
                        issue_ld(g2)
                r.T.wait(ldsem[b4 % 2], ldmark[gi])
                if gi >= 2:
                    r.T.wait(sem["v2"], qkstate["v2m"][gi - 2])
                for dt in range(16):
                    for chn in range(nch):
                        r.T.matmul(PS[b * 3 + chn][:, 0:512],
                                   wst[b4][:, dt, :],
                                   lnT[:, dt, col0 + chn * 512:col0 + (chn + 1) * 512],
                                   start=(dt == 0), stop=(dt == 15))
                r.T.mark(sem["t2"], 1)
                marks_free[gi] = sem["t2"].n
                if gi + 4 < 32:
                    r.A.wait(sem["t2"], marks_free[gi])
                    issue_ld(gi + 4)
                r.V.wait(sem["t2"])
                if gi >= 2:
                    r.V.wait(spsem[b], spmark[gi - 2])
                n = nch * 512
                for chn in range(nch):
                    r.V.tensor_copy(qsb[:, chn * 512:(chn + 1) * 512],
                                    PS[b * 3 + chn][:, 0:512])
                r.V.mark(sem["v2"], 1)
                qkstate["v2m"][gi] = sem["v2"].n
                r.V.tensor_mul(tq2[0:64, 0:n], qsb[64:128, 0:n],
                               tsin[64:128, col0:col0 + n])
                r.V.tensor_mul(tq2[64:128, 0:n], qsb[0:64, 0:n],
                               tsin[0:64, col0:col0 + n])
                r.V.tensor_mul(qsb[:, 0:n], qsb[:, 0:n], tcos[:, col0:col0 + n])
                r.V.tensor_add(ro[b][:, 0:n], qsb[:, 0:n], tq2[:, 0:n])
                r.V.mark(sem["v"], 1)
                r.S.wait(sem["v"])
                r.S.dma_start(out=spill.ap()[ot, :, 0:n], in_=ro[b][:, 0:n]
                              ).then_inc(spsem[b], 16)
                spmark[gi] = spsem[b].n
            qkstate["gi"] = gi0 + 16

        qk_sweep(c.wqT, c.q_sp, WIN, SC, 2)
        qk_sweep(c.wkT, c.k_sp, 0, WIN + SC, 3)

        vmark = {}
        spillmark = {}
        amark_v = {}
        vldsem = [sem["ld2"], sem["ld3"]]
        vldmark = {}
        r.T.wait(sem["v2"])

        def vload(i2):
            b2 = i2 % 4
            if i2 >= 4:
                r.S.wait(sem["t2"], vmark[i2 - 4])
            if i2 < 16:
                r.S.wait(*wvmark[i2])
            r.S.dma_start(out=wvsl[b2][:, :], in_=c.wv_bf.ap()[i2 % 16, :, :]
                          ).then_inc(vldsem[b2 % 2], 16)
            vldmark[i2] = vldsem[b2 % 2].n

        def vspill(s):
            r.S.wait(sem["a"], amark_v[s])
            for st2 in range(2):
                st = s * 2 + st2
                r.S.dma_start(
                    out=c.v_sp.ap()[st, :, :, :].rearrange("ot p n -> p ot n"),
                    in_=vsb[st2][:, :].rearrange("p (ot n) -> p ot n", ot=16),
                ).then_inc(sem["sp2"], 16)
            spillmark[s] = sem["sp2"].n

        vload(0)
        vload(1)
        for i in range(96):
            stg, dt = divmod(i, 16)
            if i + 2 < 96:
                vload(i + 2)
            if dt == 2 and stg > 0:
                vspill(stg - 1)
            b = i % 4
            r.T.wait(vldsem[b % 2], vldmark[i])
            if dt == 0 and stg > 0:
                r.T.wait(sem["a"], amark_v[stg - 1])
            for st2 in range(2):
                st = stg * 2 + st2
                for chk in range(4):
                    r.T.matmul(PS[st2 * 4 + chk][:, 0:512],
                               lnT[:, dt, st * 128:(st + 1) * 128],
                               wvsl[b][:, chk * 512:(chk + 1) * 512],
                               start=(dt == 0), stop=(dt == 15))
            r.T.mark(sem["t2"], 1)
            vmark[i] = sem["t2"].n
            if dt == 15:
                r.A.wait(sem["t2"], vmark[i])
                if stg >= 2:
                    r.A.wait(sem["sp2"], spillmark[stg - 2])
                for st2 in range(2):
                    for chk in range(4):
                        r.A.copy(vsb[st2][:, chk * 512:(chk + 1) * 512],
                                 PS[st2 * 4 + chk][:, 0:512])
                r.A.mark(sem["a"], 1)
                amark_v[stg] = sem["a"].n
        vspill(5)
        r.S.wait(sem["sp0"])
        r.S.wait(sem["sp1"])
        r.S.wait(sem["sp2"])
        r.replay(nc)


def phase2(c):
    nc, sem, PS = c.nc, c.sem, c.PS
    spans = [(0, 128 * (kt + 1)) for kt in range(4)] + \
            [(128 * (kt - 4), 512 - 128 * (kt - 4)) for kt in range(4, 8)]
    r = Rec()
    with ExitStack() as ph:
        outT = ph.enter_context(nc.sbuf_tensor("outT", [128, 16, SC], BF16))
        kwin = [ph.enter_context(nc.sbuf_tensor(f"kwin{i}", [128, 1024], BF16))
                for i in range(4)]
        vbl = [ph.enter_context(nc.sbuf_tensor(f"vbl{i}", [128, 8, 128], BF16))
               for i in range(4)]
        qbl = [ph.enter_context(nc.sbuf_tensor(f"qbl{i}", [128, 512], BF16))
               for i in range(4)]
        e_sb = ph.enter_context(nc.sbuf_tensor("e_sb", [128, 16, 512], BF16))
        rb = ph.enter_context(nc.sbuf_tensor("rb", [128, 512], F32))
        wst2 = [ph.enter_context(nc.sbuf_tensor(f"wst2{i}", [128, 16, 128], BF16))
                for i in range(2)]

        dqq = [ph.enter_context(nc.sbuf_tensor(f"dqq{i}", [128, 1024], F32))
               for i in range(2)]
        dqs = [ph.enter_context(nc.sbuf_tensor(f"dqs{i}", [128, 1024], F32))
               for i in range(2)]
        dqo2 = [ph.enter_context(nc.sbuf_tensor(f"dqo2{i}", [128, 1024], BF16))
                for i in range(2)]
        up_in_t = c.up_in.ap().rearrange("(a h p) n -> a h p n", a=2, h=64, p=128)
        dn_in_t = c.dn_in.ap().rearrange("(a o p) n -> a o p n", a=8, o=16, p=128)
        wupq_t = c.wupq_sh.ap().rearrange("(a p) h -> p a h", p=128)
        upsc_t = c.upsc_sh.ap().rearrange("(a p) h -> p a h", p=128)
        wdnq_t = c.wdnq_sh.ap().rearrange("(a p) o -> p a o", p=128)
        dnsc_t = c.dnsc_sh.ap().rearrange("(a p) o -> p a o", p=128)
        dinm, vdq, outm = {}, {}, {}
        womark = {}

        dqsin = [sem["din"], sem["lp0"]]
        dqsout = [sem["dqo"], sem["lp1"]]

        def dq_in(j):
            b = j % 2
            if j < 16:
                dtl, jj = j // 8, j % 8
                r.P.dma_start(out=dqq[b][:, :],
                              in_=wupq_t[:, dtl, jj * 1024:(jj + 1) * 1024]
                              ).then_inc(dqsin[b], 16)
                r.P.dma_start(out=dqs[b][:, :],
                              in_=upsc_t[:, dtl, jj * 1024:(jj + 1) * 1024]
                              ).then_inc(dqsin[b], 16)
            else:
                a, hf = (j - 16) // 2, (j - 16) % 2
                r.P.dma_start(out=dqq[b][:, :],
                              in_=wdnq_t[:, a, hf * 1024:(hf + 1) * 1024]
                              ).then_inc(dqsin[b], 16)
                r.P.dma_start(out=dqs[b][:, :],
                              in_=dnsc_t[:, a, hf * 1024:(hf + 1) * 1024]
                              ).then_inc(dqsin[b], 16)
            dinm[j] = (dqsin[b], dqsin[b].n)

        def dq_out(j):
            b = j % 2
            r.P.wait(sem["v"], vdq[j])
            if j < 16:
                dtl, jj = j // 8, j % 8
                r.P.dma_start(
                    out=up_in_t[dtl, jj * 8:(jj + 1) * 8, :, :].rearrange(
                        "h p n -> p h n"),
                    in_=dqo2[b].ap().rearrange("p (h n) -> p h n", h=8),
                ).then_inc(dqsout[b], 16)
            else:
                a, hf = (j - 16) // 2, (j - 16) % 2
                r.P.dma_start(
                    out=dn_in_t[a, hf * 8:(hf + 1) * 8, :, :].rearrange(
                        "o p n -> p o n"),
                    in_=dqo2[b].ap().rearrange("p (o n) -> p o n", o=8),
                ).then_inc(dqsout[b], 16)
            outm[j] = (dqsout[b], dqsout[b].n)

        def emit_dq(jj):
            bq2 = jj % 2
            r.V.wait(*dinm[jj])
            if jj >= 2:
                r.V.wait(*outm[jj - 2])
            r.V.scalar_tensor_tensor(dqo2[bq2][:, :], dqq[bq2][:, :], -7.5,
                                     dqs[bq2][:, :], op0=ALU.add, op1=ALU.mult)
            r.V.mark(sem["v"], 1)
            vdq[jj] = sem["v"].n
            dq_out(jj)
            if jj + 2 < 32:
                dq_in(jj + 2)

        dq_in(0)
        dq_in(1)

        r.V.memset(e_sb[:, :, :], 0.0)
        r.V.mark(sem["v"], 1)
        r.T.wait(sem["v"])
        r.A.wait(sem["v"])

        it = [(blk, h) for blk in range(2) for h in range(16)]
        ldmark = {}
        ldsem = [sem["ld0"], sem["ld1"]]

        def stream_ld(j):
            bj = j % 4
            bl, hh = it[j]
            r.S.dma_start(out=kwin[bj][:, :],
                          in_=c.k_sp.ap()[hh, :, bl * 512:bl * 512 + 1024]
                          ).then_inc(ldsem[bj % 2], 16)
            r.S.dma_start(out=vbl[bj][:, :, :],
                          in_=c.v_sp.ap()[4 * bl:4 * bl + 8, hh, :, :]
                          .rearrange("st p n -> p st n")).then_inc(ldsem[bj % 2], 16)
            r.S.dma_start(out=qbl[bj][:, :],
                          in_=c.q_sp.ap()[hh, :, bl * 512:bl * 512 + 512]
                          ).then_inc(ldsem[bj % 2], 16)
            ldmark[j] = ldsem[bj % 2].n

        for j in (0, 1, 2, 3):
            stream_ld(j)

        STB = [0, 1, 3, 4, 5]
        sTm = {}
        expA = {}
        trim = {}
        avm = {}
        tailv = {}
        for i, (blk, h) in enumerate(it):
            b = i % 4
            es = (i % 2) * 8
            r.T.wait(ldsem[b % 2], ldmark[i])

            def sum_av(lag):
                r.T.wait(sem["v"], trim[i * 8 + lag])
                r.T.matmul(PS[6][0:1, 0:512], c.ones_col[:, :],
                           e_sb[:, es + lag, :], start=(lag == 0),
                           stop=(lag == 7))
                r.T.matmul(PS[7][:, 0:512], vbl[b][:, lag, :],
                           e_sb[:, es + lag, :], start=(lag == 0),
                           stop=(lag == 7)).then_inc(sem["t2"], 1)

            for kt in range(8):
                gi = i * 8 + kt
                qoff, n = spans[kt]
                pb = PS[STB[gi % 5]]
                if gi - 5 in expA:
                    r.T.wait(sem["a"], expA[gi - 5])
                r.T.matmul(pb[:, 0:n], kwin[b][:, 128 * kt:128 * kt + 128],
                           qbl[b][:, qoff:qoff + n], start=True, stop=True
                           ).then_inc(sem["t"], 1)
                sTm[gi] = sem["t"].n
                if i > 1 and kt == 0:
                    r.A.wait(sem["t2"], avm[i - 2])
                r.A.wait(sem["t"], sTm[gi])
                r.A.activation(e_sb[:, es + kt, qoff:qoff + n], pb[:, 0:n],
                               AF.Exp, scale=float(ISQH),
                               bias=c.zadd_a[:, blk * 8 + kt:blk * 8 + kt + 1])
                r.A.mark(sem["a"], 1)
                expA[gi] = sem["a"].n
                bq = 128 * kt if kt < 4 else qoff
                tri = c.tri_prev if kt < 4 else c.tri_cur
                r.V.wait(sem["a"], expA[gi])
                r.V.tensor_mul(e_sb[:, es + kt, bq:bq + 128],
                               e_sb[:, es + kt, bq:bq + 128],
                               tri[:, :]).then_inc(sem["v"], 1)
                trim[gi] = sem["v"].n
                if i < 16 and kt in (1, 4):
                    emit_dq(2 * i + (0 if kt == 1 else 1))
                if kt >= 4:
                    lag = kt - 4
                    if lag == 0 and i > 0:
                        r.T.wait(sem["v2"], tailv[i - 1])
                    sum_av(lag)
            for lag in (4, 5, 6, 7):
                sum_av(lag)
            avm[i] = sem["t2"].n
            if i + 4 < len(it):
                r.S.wait(sem["t2"], avm[i])
                stream_ld(i + 4)
            r.V.wait(sem["t2"], avm[i])
            r.V.tensor_copy(c.rows_bf[0:1, 0:512], PS[6][0:1, 0:512]
                            ).then_inc(sem["v2"], 1)
            r.T.wait(sem["v2"])
            r.T.matmul(PS[2][:, 0:512], c.ones_rb[:, :], c.rows_bf[0:1, 0:512],
                       start=True, stop=True).then_inc(sem["t"], 1)
            r.A.wait(sem["t"])
            r.A.activation(rb[:, :], PS[2][:, 0:512], AF.Ln)
            r.A.activation(rb[:, :], rb[:, :], AF.Exp, scale=-1.0)
            r.A.mark(sem["a"], 1)
            r.V.wait(sem["a"])
            r.V.scalar_tensor_tensor(
                outT[:, h, blk * 512:blk * 512 + 512], PS[7][:, 0:512], 1.0,
                rb[:, :], op0=ALU.mult, op1=ALU.mult)
            r.V.mark(sem["v2"], 1)
            tailv[i] = sem["v2"].n
            if i == 7:
                r.P.wait(*outm[14])
                r.P.wait(*outm[15])
                r.P.collective_compute(
                    "AllGather", ALU.bypass, replica_groups=[list(range(NC))],
                    ins=[c.up_in.ap().opt()], outs=[c.up_out.ap().opt()],
                ).then_inc(sem["ccu"], 1)
            if i == 15:
                r.P.wait(*outm[30])
                r.P.wait(*outm[31])
                r.P.collective_compute(
                    "AllGather", ALU.bypass, replica_groups=[list(range(NC))],
                    ins=[c.dn_in.ap().opt()], outs=[c.dn_out.ap().opt()],
                ).then_inc(sem["ccd"], 1)
                woT_t2 = c.woT.ap().rearrange("(dt p) o -> p dt o", p=128)
                womark.update(prep_chain(
                    c, r, dqo2,
                    [(woT_t2[:, :, j * 64:(j + 1) * 64],
                      c.wo_bf.ap()[:, :, j * 64:(j + 1) * 64].rearrange(
                          "dt p n -> p dt n"), None, 16)
                     for j in range(32)]))
        wo_ldm = {}
        wosem = [sem["ld2"], sem["ld3"]]

        def wst2_load(ot2, b2):
            r.S.wait(*womark[2 * ot2])
            r.S.wait(*womark[2 * ot2 + 1])
            r.S.dma_start(out=wst2[b2][:, :, :],
                          in_=c.wo_bf.ap()[:, :, ot2 * 128:(ot2 + 1) * 128]
                          .rearrange("dt p n -> p dt n")).then_inc(wosem[b2], 16)
            wo_ldm[ot2] = wosem[b2].n

        for ot in range(2):
            wst2_load(ot, ot)
        r.T.wait(sem["v2"])
        for ot in range(16):
            b = ot % 2
            r.T.wait(wosem[b], wo_ldm[ot])
            if ot >= 2:
                r.T.wait(sem["v"])
            for chn in range(2):
                pb = PS[b * 2 + chn]
                for hdt in range(16):
                    r.T.matmul(pb[:, 0:512], wst2[b][:, hdt, :],
                               outT[:, hdt, chn * 512:(chn + 1) * 512],
                               start=(hdt == 0), stop=(hdt == 15))
            r.T.mark(sem["t"], 1)
            if ot + 2 < 16:
                r.S.wait(sem["t"])
                wst2_load(ot + 2, b)
            r.V.wait(sem["t"])
            for chn in range(2):
                pb = PS[b * 2 + chn]
                r.V.scalar_tensor_tensor(
                    c.xT[:, ot, chn * 512:(chn + 1) * 512], pb[:, 0:512], 1.0,
                    c.xT[:, ot, chn * 512:(chn + 1) * 512],
                    op0=ALU.mult, op1=ALU.add)
            r.V.mark(sem["v"], 1)
        r.replay(nc)


def ffn_phase(c, inst):
    nc, sem, PS = c.nc, c.sem, c.PS
    up_out_t = c.up_out.ap().rearrange("(g h p) n -> g h p n", g=NC * 2, p=128)
    dn_out_t = c.dn_out.ap().rearrange("(g o p) n -> g o p n", g=NC * 8, p=128)
    r = Rec()
    with ExitStack() as ph:
        A1b = ph.enter_context(nc.sbuf_tensor(f"A1b{inst}", [128, 16, R], BF16))
        A2b = ph.enter_context(nc.sbuf_tensor(f"A2b{inst}", [128, 64, R], BF16))
        B2b = ph.enter_context(nc.sbuf_tensor(f"B2b{inst}", [R, D], BF16))
        bup = ph.enter_context(nc.sbuf_tensor(f"bup{inst}", [128, 64], F32))
        bdn = ph.enter_context(nc.sbuf_tensor(f"bdn{inst}", [128, 16], F32))
        b1st = [ph.enter_context(nc.sbuf_tensor(f"b1st{inst}{i}", [R, 512], BF16))
                for i in range(2)]
        r.P.dma_start(out=A1b[:, :, :],
                      in_=c.loraA1.ap().rearrange("(dt p) q -> p dt q", p=128)
                      ).then_inc(sem["onep"], 16)
        r.P.dma_start(out=A2b[:, :, :],
                      in_=c.loraA2.ap().rearrange("(ht p) q -> p ht q", p=128)
                      ).then_inc(sem["onep"], 16)
        r.P.dma_start(out=B2b[:, :], in_=c.loraB2[:, :]).then_inc(sem["onep"], 16)
        r.S.dma_start(out=bup[:, :], in_=c.bup_i[:, :]).then_inc(sem["one"], 16)
        r.S.dma_start(out=bdn[:, :], in_=c.bdn_i[:, :]).then_inc(sem["one"], 16)

        lnT = ph.enter_context(nc.sbuf_tensor(f"lnTf{inst}", [128, 16, SC], BF16))
        x2T = ph.enter_context(nc.sbuf_tensor(f"x2T{inst}", [128, 64, 512], BF16))
        x2flat = x2T.ap().rearrange("p a n -> p (a n)")
        xbf = x2flat[:, 16384:24576]
        fsc = ph.enter_context(nc.sbuf_tensor(f"fsc_f{inst}", [128, 512], F32))
        wup = [ph.enter_context(nc.sbuf_tensor(f"wup{inst}{i}", [128, 16, 128], BF16))
               for i in range(2)]
        wdn = [ph.enter_context(nc.sbuf_tensor(f"wdn{inst}{i}", [128, 16, 128], BF16))
               for i in range(2)]
        tT = ph.enter_context(nc.sbuf_tensor(f"tT{inst}", [16, SC], BF16))
        t2T = ph.enter_context(nc.sbuf_tensor(f"t2T{inst}", [16, 512], BF16))

        if inst == 0:
            b1tmp = x2T.ap().rearrange("p a n -> p (a n)")[0:R, 0:HF]
            r.P.dma_start(out=b1tmp, in_=c.loraB1[:, :]
                          ).then_inc(sem["din"], 16)
            r.P.wait(sem["din"])
            r.P.dma_start(out=c.b1_bf.ap(), in_=b1tmp
                          ).then_inc(sem["dqo"], 16)

        def ln_chunk(h2):
            return emit_ln(
                c, r,
                src=lambda dt, c0, n: c.xT[:, dt, h2 * 512 + c0:h2 * 512 + c0 + n],
                dst=lambda dt, c0, n: lnT[:, dt, h2 * 512 + c0:h2 * 512 + c0 + n],
                ncols=512, xbf=xbf, fsc=fsc,
                pbS=PS[7], pbR=PS[5], pbM=PS[4], tail_wait=(h2 == 0))

        def lora_up(chn):
            for dt in range(16):
                r.T.matmul(PS[6][0:16, 0:512], A1b[:, dt, :],
                           lnT[:, dt, chn * 512:(chn + 1) * 512],
                           start=(dt == 0), stop=(dt == 15))
            r.T.mark(sem["t"], 1)
            r.V.wait(sem["t"])
            r.V.tensor_copy(tT[0:16, chn * 512:(chn + 1) * 512], PS[6][0:16, 0:512]
                            ).then_inc(sem["v"], 1)
            r.T.wait(sem["v"])

        ln_chunk(0)
        r.T.wait(sem["one"])
        r.T.wait(sem["onep"])
        lora_up(0)
        ln_norm1 = ln_chunk(1)

        relu_mark = {}
        evac_mark = {}
        for half in range(2):
            s0 = half * 512
            if half == 1:
                r.T.wait(sem["v"], ln_norm1)
                lora_up(1)
            upmark = {}
            b1mark = {}
            b1used = {}
            upsem = [sem["ld0"], sem["ld1"]]
            b1sem = [sem["ld2"], sem["ld3"]]
            for pre in range(2):
                if half == 0 and pre == 0:
                    r.S.wait(sem["ccu"])
                if half == 1:
                    r.S.wait(sem["a"])
                r.S.dma_start(out=wup[pre][:, :, :],
                              in_=up_out_t[:, pre, :, :].rearrange("g p n -> p g n")
                              ).then_inc(upsem[pre], 16)
                upmark[pre] = upsem[pre].n
                if half == 0 and inst == 0 and pre == 0:
                    r.S.wait(sem["dqo"])
                r.S.dma_start(out=b1st[pre][:, :],
                              in_=c.b1_bf.ap()[:, pre * 512:(pre + 1) * 512]
                              ).then_inc(b1sem[pre], 16)
                b1mark[pre] = b1sem[pre].n
            for ht in range(64):
                b = ht % 2
                bank = ht % 4
                grp, sl = divmod(ht, 4)
                pb = PS[bank]
                r.T.wait(upsem[b], upmark[ht])
                r.T.wait(b1sem[grp % 2], b1mark[grp])
                if bank in relu_mark:
                    r.T.wait(sem["a"], relu_mark[bank])
                for dt in range(16):
                    r.T.matmul(pb[:, 0:512], wup[b][:, dt, :],
                               lnT[:, dt, s0:s0 + 512],
                               start=(dt == 0), stop=False)
                r.T.matmul(pb[:, 0:512], b1st[grp % 2][0:16, sl * 128:(sl + 1) * 128],
                           tT[0:16, s0:s0 + 512], start=False, stop=True
                           ).then_inc(sem["t"], 1)
                b1used[grp] = sem["t"].n
                if ht + 2 < 64:
                    r.S.wait(sem["t"])
                    r.S.dma_start(out=wup[b][:, :, :],
                                  in_=up_out_t[:, ht + 2, :, :].rearrange("g p n -> p g n")
                                  ).then_inc(upsem[b], 16)
                    upmark[ht + 2] = upsem[b].n
                if sl == 3 and grp + 2 < 16:
                    r.S.wait(sem["t"], b1used[grp])
                    r.S.dma_start(out=b1st[grp % 2][:, :],
                                  in_=c.b1_bf.ap()[:, (grp + 2) * 512:(grp + 3) * 512]
                                  ).then_inc(b1sem[grp % 2], 16)
                    b1mark[grp + 2] = b1sem[grp % 2].n
                r.A.wait(sem["t"])
                r.A.activation(x2T[:, ht, :], pb[:, 0:512], AF.Relu,
                               bias=bup[:, ht:ht + 1])
                r.A.mark(sem["a"], 1)
                relu_mark[bank] = sem["a"].n
            r.T.wait(sem["a"])
            for ht in range(64):
                r.T.matmul(PS[6][0:16, 0:512], A2b[:, ht, :], x2T[:, ht, :],
                           start=(ht == 0), stop=(ht == 63))
            r.T.mark(sem["t"], 1)
            r.V.wait(sem["t"])
            r.V.tensor_copy(t2T[0:16, 0:512], PS[6][0:16, 0:512]).then_inc(sem["v"], 1)
            dnmark = {}
            stage_done = {}
            dnsem = [sem["ld4"], sem["ld5"]]
            for pre in range(2):
                if half == 0 and pre == 0:
                    r.S.wait(sem["ccd"])
                if half == 1:
                    r.S.wait(sem["v"])
                r.S.dma_start(out=wdn[pre][:, :, :],
                              in_=dn_out_t[pre * 16:(pre + 1) * 16, 0, :, :]
                              .rearrange("g p n -> p g n")).then_inc(dnsem[pre], 16)
                dnmark[pre] = dnsem[pre].n
            r.T.wait(sem["v"])
            if half == 0:
                r.T.wait(sem["v"], ln_norm1)
            for ot in range(16):
                bank = 4 + ot % 2
                pb = PS[bank]
                if bank in evac_mark:
                    r.T.wait(sem["v"], evac_mark[bank])
                for hw4 in range(4):
                    stg = ot * 4 + hw4
                    r.T.wait(dnsem[hw4 % 2], dnmark[stg])
                    for hh in range(16):
                        r.T.matmul(pb[:, 0:512], wdn[hw4 % 2][:, hh, :],
                                   x2T[:, hw4 * 16 + hh, :],
                                   start=(hw4 == 0 and hh == 0), stop=False)
                    r.T.mark(sem["t2"], 1)
                    stage_done[stg] = sem["t2"].n
                    nxt = stg + 2
                    if nxt < 64:
                        n_ot, n_hw = divmod(nxt, 4)
                        r.S.wait(sem["t2"], stage_done[stg])
                        r.S.dma_start(
                            out=wdn[hw4 % 2][:, :, :],
                            in_=dn_out_t[n_hw * 16:(n_hw + 1) * 16, n_ot, :, :]
                            .rearrange("g p n -> p g n")).then_inc(dnsem[hw4 % 2], 16)
                        dnmark[nxt] = dnsem[hw4 % 2].n
                r.T.matmul(pb[:, 0:512], B2b[0:16, ot * 128:(ot + 1) * 128],
                           t2T[0:16, 0:512], start=False, stop=True
                           ).then_inc(sem["t"], 1)
                r.V.wait(sem["t"])
                r.V.scalar_tensor_tensor(
                    c.xT[:, ot, s0:s0 + 512], pb[:, 0:512],
                    bdn[:, ot:ot + 1], c.xT[:, ot, s0:s0 + 512],
                    op0=ALU.add, op1=ALU.add)
                r.V.mark(sem["v"], 1)
                evac_mark[bank] = sem["v"].n
                if inst == 1:
                    r.S.wait(sem["v"], evac_mark[bank])
                    r.S.dma_start(
                        out=c.out_xT.ap().rearrange("(dt p) n -> p dt n", p=128)
                        [:, ot, s0:s0 + 512],
                        in_=c.xT[:, ot, s0:s0 + 512]).then_inc(sem["dout"], 16)
                    c.out_streamed = True
        r.replay(nc)


def phase4(c, sub="full"):
    nc, sem, PS = c.nc, c.sem, c.PS
    r = Rec()
    k_out_t = c.k_out.ap().rearrange("(r t) n -> r t n", r=NC)
    v_out_t = c.v_out.ap().rearrange("(r t) n -> r t n", r=NC)
    with ExitStack() as ph:
        wkvb = ph.enter_context(nc.sbuf_tensor("wkvb", [128, 16, DL], BF16))
        wcb = ph.enter_context(nc.sbuf_tensor("wcb", [128, 4, DC], BF16))
        wqcb = ph.enter_context(nc.sbuf_tensor("wqcb", [128, 2, DC], BF16))
        wkvcb = ph.enter_context(nc.sbuf_tensor("wkvcb", [128, 2, 2 * DC], BF16))
        wc2lb = ph.enter_context(nc.sbuf_tensor("wc2lb", [128, 2, DL], BF16))
        gb = ph.enter_context(nc.sbuf_tensor("gbs", [128, 4], F32))
        bor = ph.enter_context(nc.sbuf_tensor("bors", [128, 16], F32))
        cosb = ph.enter_context(nc.sbuf_tensor("cosb", [128, 2, SC], F32))
        sinb = ph.enter_context(nc.sbuf_tensor("sinb", [128, 2, SC], F32))
        qTr = ph.enter_context(nc.sbuf_tensor("qTr", [128, 2, SC], BF16))
        kTr = ph.enter_context(nc.sbuf_tensor("kTr", [128, 2, SC], BF16))
        vN = ph.enter_context(nc.sbuf_tensor("vN", [128, 8, DC], BF16))
        o_sb = ph.enter_context(nc.sbuf_tensor("o_sb", [128, 2, SC], BF16))

        for dst, src, p in [
            (wkvb[:, :, :], c.wkvT.ap().rearrange("(a p) n -> p a n", p=128), 1),
            (wcb[:, :, :], c.wcT.ap().rearrange("(a p) n -> p a n", p=128), 1),
            (wqcb[:, :, :], c.wqcT.ap().rearrange("(a p) n -> p a n", p=128), 1),
            (wkvcb[:, :, :], c.wkvcT.ap().rearrange("(a p) n -> p a n", p=128), 1),
            (wc2lb[:, :, :], c.wc2lT.ap().rearrange("(a p) n -> p a n", p=128), 1),
        ]:
            r.P.dma_start(out=dst, in_=src).then_inc(sem["onep"], 16)
        r.S.dma_start(out=gb[:, :], in_=c.gb_i[:, :]).then_inc(sem["one"], 16)
        r.S.dma_start(out=bor[:, :], in_=c.bor_i[:, :]).then_inc(sem["one"], 16)
        r.S.dma_start(out=cosb[:, :, :],
                      in_=c.tb_cos.ap().rearrange("(a p) n -> p a n", p=128)
                      ).then_inc(sem["one"], 16)
        r.S.dma_start(out=sinb[:, :, :],
                      in_=c.tb_sin.ap().rearrange("(a p) n -> p a n", p=128)
                      ).then_inc(sem["one"], 16)
        r.V.wait(sem["one"])

        kvl = ph.enter_context(nc.sbuf_tensor("kvl", [128, 4, SC], BF16))
        with ExitStack() as ph2:
            lnT = ph2.enter_context(nc.sbuf_tensor("lnTb", [128, 16, SC], BF16))
            xbf = ph2.enter_context(nc.sbuf_tensor("xbf_b", [128, 16 * 512], BF16))
            fsc = ph2.enter_context(nc.sbuf_tensor("fsc_b", [128, 512], F32))
            emit_ln(c, r,
                    src=lambda dt, c0, n: c.xT[:, dt, c0:c0 + n],
                    dst=lambda dt, c0, n: lnT[:, dt, c0:c0 + n],
                    ncols=SC, xbf=xbf, fsc=fsc,
                    pbS=PS[6], pbR=PS[7], pbM=PS[5],
                    alt=(PS[2], PS[3], PS[4]))
            r.T.wait(sem["onep"])
            for lt in range(4):
                for chn in range(2):
                    pb = PS[lt % 2]
                    for dt in range(16):
                        r.T.matmul(pb[:, 0:512], wkvb[:, dt, lt * 128:(lt + 1) * 128],
                                   lnT[:, dt, chn * 512:(chn + 1) * 512],
                                   start=(dt == 0), stop=(dt == 15))
                    r.T.mark(sem["t"], 1)
                    r.V.wait(sem["t"])
                    r.V.tensor_copy(kvl[:, lt, chn * 512:(chn + 1) * 512],
                                    pb[:, 0:512]).then_inc(sem["v"], 1)
                    r.T.wait(sem["v"])
            r.replay(nc)

        r = Rec()
        ph3 = ph.enter_context(ExitStack())
        c_sb = ph3.enter_context(nc.sbuf_tensor("c_sb", [128, 2, SC], F32))
        chT = ph3.enter_context(nc.sbuf_tensor("chT", [128, 2, SC], BF16))
        vTs = ph3.enter_context(nc.sbuf_tensor("vTs", [128, 2, SC], BF16))
        qf = ph3.enter_context(nc.sbuf_tensor("qf", [128, 2, SC], F32))
        tr1 = ph3.enter_context(nc.sbuf_tensor("tr1", [128, SC], F32))
        tr2 = ph3.enter_context(nc.sbuf_tensor("tr2", [128, SC], F32))

        for dct in range(2):
            for chn in range(2):
                pb = PS[dct]
                for lt in range(4):
                    r.T.matmul(pb[:, 0:512], wcb[:, lt, dct * 128:(dct + 1) * 128],
                               kvl[:, lt, chn * 512:(chn + 1) * 512],
                               start=(lt == 0), stop=(lt == 3))
                r.T.mark(sem["t"], 1)
                r.V.wait(sem["t"])
                r.V.tensor_copy(c_sb[:, dct, chn * 512:(chn + 1) * 512], pb[:, 0:512]
                                ).then_inc(sem["v"], 1)
                r.T.wait(sem["v"])
        cbf = ph3.enter_context(nc.sbuf_tensor("cbf", [128, 2, 512], BF16))
        for chn in range(2):
            n0 = chn * 512
            for dct in range(2):
                r.V.tensor_copy(cbf[:, dct, :], c_sb[:, dct, n0:n0 + 512])
            r.V.mark(sem["v"], 1)
            r.T.wait(sem["v"])
            for dct in range(2):
                r.T.matmul(PS[6][0:1, 0:512], c.ones_col[:, :],
                           cbf[:, dct, :], start=(dct == 0), stop=(dct == 1))
            r.T.mark(sem["t"], 1)
            r.A.wait(sem["t"])
            r.A.activation(cbf[:, :, :], cbf[:, :, :], AF.Square)
            r.A.mark(sem["a"], 1)
            r.T.wait(sem["a"])
            for dct in range(2):
                r.T.matmul(PS[6][32:33, 0:512], c.ones_col[:, :],
                           cbf[:, dct, :], start=(dct == 0), stop=(dct == 1))
            r.T.mark(sem["t"], 1)
            r.V.wait(sem["t"])
            r.V.tensor_scalar_mul(c.rows[0:1, 0:512], PS[6][0:1, 0:512], 1.0 / DC)
            r.V.tensor_scalar_mul(c.rows[0:1, 512:1024], PS[6][32:33, 0:512], 1.0 / DC)
            r.V.tensor_mul(c.rows[0:1, 1024:1536], c.rows[0:1, 0:512], c.rows[0:1, 0:512])
            r.V.tensor_sub(c.rows[0:1, 512:1024], c.rows[0:1, 512:1024],
                           c.rows[0:1, 1024:1536])
            r.V.tensor_scalar_add(c.rows[0:1, 512:1024], c.rows[0:1, 512:1024], EPS)
            r.V.mark(sem["v"], 1)
            r.A.wait(sem["v"])
            r.A.activation(c.rows[0:1, 512:1024], c.rows[0:1, 512:1024], AF.Ln)
            r.A.activation(c.rows[0:1, 512:1024], c.rows[0:1, 512:1024], AF.Exp, scale=-0.5)
            r.A.mark(sem["a"], 1)
            r.V.wait(sem["a"])
            r.V.tensor_mul(c.rows[0:1, 1024:1536], c.rows[0:1, 0:512], c.rows[0:1, 512:1024])
            r.V.mark(sem["v"], 1)
            r.T.wait(sem["v"])
            r.T.matmul(PS[7][:, 0:512], c.ones_row[:, :], c.rows[0:1, 512:1024],
                       start=True, stop=True)
            r.T.matmul(PS[5][:, 0:512], c.ones_row[:, :], c.rows[0:1, 1024:1536],
                       start=True, stop=True).then_inc(sem["t"], 1)
            r.V.wait(sem["t"])
            for dct in range(2):
                r.V.tensor_mul(tr1[:, 0:512], c_sb[:, dct, n0:n0 + 512], PS[7][:, 0:512])
                r.V.tensor_tensor(tr2[:, 0:512], tr1[:, 0:512], PS[5][:, 0:512],
                                  op=ALU.subtract)
                r.V.tensor_scalar(chT[:, dct, n0:n0 + 512], tr2[:, 0:512],
                                  gb[:, dct:dct + 1], gb[:, 2 + dct:3 + dct],
                                  op0=ALU.mult, op1=ALU.add)
            r.V.mark(sem["v"], 1)
            r.T.wait(sem["v"])

        def proj2(dst_f32, wt, otile, nout_tiles):
            for t in range(nout_tiles):
                for chn in range(2):
                    pb = PS[t % 2]
                    for dct in range(2):
                        r.T.matmul(pb[:, 0:512],
                                   wt[:, dct, (otile + t) * 128:(otile + t + 1) * 128],
                                   chT[:, dct, chn * 512:(chn + 1) * 512],
                                   start=(dct == 0), stop=(dct == 1))
                    r.T.mark(sem["t"], 1)
                    r.V.wait(sem["t"])
                    r.V.tensor_copy(dst_f32[:, t, chn * 512:(chn + 1) * 512],
                                    pb[:, 0:512]).then_inc(sem["v"], 1)
                    r.T.wait(sem["v"])

        def ropeb(dst_bf, src_f32):
            for dct in range(2):
                r.V.tensor_mul(tr1[:, 0:SC], src_f32[:, dct, :], cosb[:, dct, :])
                r.V.tensor_mul(tr2[:, 0:SC], src_f32[:, 1 - dct, :], sinb[:, dct, :])
                r.V.tensor_add(dst_bf[:, dct, :], tr1[:, 0:SC], tr2[:, 0:SC])
            r.V.mark(sem["v"], 1)

        proj2(qf, wqcb, 0, 2)
        ropeb(qTr, qf)
        proj2(qf, wkvcb, 0, 2)
        ropeb(kTr, qf)
        r.S.wait(sem["v"])
        r.S.dma_start(out=c.kv_in.ap()[0:256, :].rearrange(
                          "(a p) n -> p a n", p=128),
                      in_=kTr[:, :, :]).then_inc(sem["sp0"], 16)
        r.P.wait(sem["sp0"])
        r.P.collective_compute(
            "AllGather", ALU.bypass, replica_groups=[list(range(NC))],
            ins=[c.kv_in.ap()[0:256, :].opt()], outs=[c.k_out.ap().opt()],
        ).then_inc(sem["cck"], 1)
        proj2(qf, wkvcb, 2, 2)
        for dct in range(2):
            r.V.tensor_copy(vTs[:, dct, :], qf[:, dct, :])
        r.V.mark(sem["v"], 1)
        r.T.wait(sem["v"])
        for st in range(8):
            for dct in range(2):
                r.T.transpose(PS[st % 2].ap().bitcast(BF16)[:, dct * 128:dct * 128 + 128],
                              vTs[:, dct, st * 128:(st + 1) * 128], c.ident_bf[:, :])
            r.T.mark(sem["t"], 1)
            r.V.wait(sem["t"])
            r.V.tensor_copy(vN[:, st, :], PS[st % 2].ap().bitcast(BF16)[:, 0:256]
                            ).then_inc(sem["v"], 1)
            r.T.wait(sem["v"])
        r.S.wait(sem["v"])
        for st in range(8):
            r.S.dma_start(out=c.kv_in.ap()[256 + 32 * st:256 + 32 * (st + 1), :],
                          in_=vN[:, st, :]).then_inc(sem["sp0"], 16)
        r.P.wait(sem["sp0"])
        r.P.collective_compute(
            "AllGather", ALU.bypass, replica_groups=[list(range(NC))],
            ins=[c.kv_in.ap()[256:512, :].opt()], outs=[c.v_out.ap().opt()],
        ).then_inc(sem["cck2"], 1)
        if sub == "kv":
            r.P.wait(sem["cck"])
            r.P.wait(sem["cck2"])
            r.replay(nc)
            ph3.close()
            return

        r.replay(nc)
        ph3.close()
        r = Rec()
        worb = ph.enter_context(nc.sbuf_tensor("worb", [128, 4, D], BF16))
        wkvnb = ph.enter_context(nc.sbuf_tensor("wkvnb", [128, 4, D], BF16))
        r.P.dma_start(out=worb[:, :, :],
                      in_=c.worT.ap().rearrange("(a p) n -> p a n", p=128)
                      ).then_inc(sem["onep"], 16)
        r.P.dma_start(out=wkvnb[:, :, :],
                      in_=c.wkv_nat.ap().rearrange("(a p) n -> p a n", p=128)
                      ).then_inc(sem["onep"], 16)
        kwr = [ph.enter_context(nc.sbuf_tensor(f"kwr{i}", [128, 2, SC], BF16))
               for i in range(2)]
        vwr = [ph.enter_context(nc.sbuf_tensor(f"vwr{i}", [128, 8, DC], BF16))
               for i in range(2)]
        e_b = ph.enter_context(nc.sbuf_tensor("e_b", [128, 8, 512], BF16))
        rbb = ph.enter_context(nc.sbuf_tensor("rbb", [128, 512], F32))
        zT = ph.enter_context(nc.sbuf_tensor("zTb", [128, 4, SC], BF16))

        r.S.wait(sem["cck"])
        STB4 = [0, 1, 4, 5]
        vwaited = [False]
        for g in range(2):
            q0 = g * 512
            ldm = {}
            ktsem = [sem["ld2"], sem["ld3"]]
            for pre in range(2):
                if g > 0:
                    r.S.wait(sem["t2"])
                r.S.dma_start(out=kwr[pre][:, :, :],
                              in_=k_out_t[pre, :, :]
                              .rearrange("(a p) n -> p a n", p=128)
                              ).then_inc(ktsem[pre], 16)
            if not vwaited[0]:
                r.S.wait(sem["cck2"])
                vwaited[0] = True
            for pre in range(2):
                r.S.dma_start(out=vwr[pre][:, :, :],
                              in_=v_out_t[pre, :, :].rearrange(
                                  "(st pr) (pq dc) -> (pr pq) st dc", st=8, pq=4)
                              ).then_inc(ktsem[pre], 16)
                ldm[pre] = ktsem[pre].n
            sTm = {}
            expA = {}
            avm2 = {}
            for rr in range(NC):
                r.T.wait(ktsem[rr % 2], ldm[rr])
                for ktl in range(8):
                    ktg = rr * 8 + ktl
                    pb = PS[STB4[ktg % 4]]
                    if ktg - 4 in expA:
                        r.T.wait(sem["a"], expA[ktg - 4])
                    if ktg == 0 and g > 0:
                        r.T.wait(sem["v2"])
                    for dct in range(2):
                        r.T.matmul(pb[:, 0:512],
                                   kwr[rr % 2][:, dct, ktl * 128:(ktl + 1) * 128],
                                   qTr[:, dct, q0:q0 + 512],
                                   start=(dct == 0), stop=(dct == 1))
                    r.T.mark(sem["t"], 1)
                    sTm[ktg] = sem["t"].n
                    if ktg - 8 >= 0:
                        r.A.wait(sem["t2"], avm2.get(ktg - 8, 0))
                    r.A.wait(sem["t"], sTm[ktg])
                    r.A.activation(e_b[:, ktg % 8, :], pb[:, 0:512], AF.Exp,
                                   scale=float(ISQC),
                                   bias=c.zadd_b[:, g * 64 + ktg:g * 64 + ktg + 1])
                    r.A.mark(sem["a"], 1)
                    expA[ktg] = sem["a"].n
                    if ktg >= 4:
                        lag = ktg - 4
                        r.T.wait(sem["a"], expA[lag])
                        r.T.matmul(PS[6][0:1, 0:512], c.ones_col[:, :],
                                   e_b[:, lag % 8, :], start=(lag == 0), stop=False)
                        r.T.matmul(PS[7][:, 0:512],
                                   vwr[(lag // 8) % 2][:, lag % 8, 0:128],
                                   e_b[:, lag % 8, :], start=(lag == 0), stop=False)
                        r.T.matmul(PS[3][:, 0:512],
                                   vwr[(lag // 8) % 2][:, lag % 8, 128:256],
                                   e_b[:, lag % 8, :], start=(lag == 0), stop=False
                                   ).then_inc(sem["t2"], 1)
                        avm2[lag] = sem["t2"].n
                if rr + 2 < NC:
                    r.S.wait(sem["t2"])
                    r.S.dma_start(out=kwr[rr % 2][:, :, :],
                                  in_=k_out_t[rr + 2, :, :]
                                  .rearrange("(a p) n -> p a n", p=128)
                                  ).then_inc(ktsem[rr % 2], 16)
                    r.S.dma_start(out=vwr[rr % 2][:, :, :],
                                  in_=v_out_t[rr + 2, :, :].rearrange(
                                      "(st pr) (pq dc) -> (pr pq) st dc", st=8, pq=4)
                                  ).then_inc(ktsem[rr % 2], 16)
                    ldm[rr + 2] = ktsem[rr % 2].n
            for lag in (60, 61, 62, 63):
                r.T.wait(sem["a"], expA[lag])
                r.T.matmul(PS[6][0:1, 0:512], c.ones_col[:, :],
                           e_b[:, lag % 8, :], start=False, stop=False)
                r.T.matmul(PS[7][:, 0:512], vwr[(lag // 8) % 2][:, lag % 8, 0:128],
                           e_b[:, lag % 8, :], start=False, stop=False)
                r.T.matmul(PS[3][:, 0:512], vwr[(lag // 8) % 2][:, lag % 8, 128:256],
                           e_b[:, lag % 8, :], start=False, stop=False
                           ).then_inc(sem["t2"], 1)
                avm2[lag] = sem["t2"].n
            for i2 in range(4):
                og = 64 + i2
                pb = PS[STB4[og % 4]]
                if og - 4 in expA:
                    r.T.wait(sem["a"], expA[og - 4])
                for dct in range(2):
                    r.T.matmul(pb[:, 0:512],
                               kTr[:, dct, q0 + i2 * 128:q0 + (i2 + 1) * 128],
                               qTr[:, dct, q0:q0 + 512],
                               start=(dct == 0), stop=(dct == 1))
                r.T.mark(sem["t"], 1)
                sTm[og] = sem["t"].n
                r.A.wait(sem["t2"], avm2.get(og - 8, 0))
                r.A.wait(sem["t"], sTm[og])
                r.A.activation(e_b[:, og % 8, :], pb[:, 0:512], AF.Exp,
                               scale=float(ISQC))
                r.A.mark(sem["a"], 1)
                expA[og] = sem["a"].n
                r.V.wait(sem["a"], expA[og])
                for j2 in range(4):
                    if i2 == j2:
                        r.V.tensor_mul(e_b[:, og % 8, j2 * 128:(j2 + 1) * 128],
                                       e_b[:, og % 8, j2 * 128:(j2 + 1) * 128],
                                       c.tri_cur[:, :])
                    elif i2 > j2:
                        r.V.tensor_scalar_mul(
                            e_b[:, og % 8, j2 * 128:(j2 + 1) * 128],
                            e_b[:, og % 8, j2 * 128:(j2 + 1) * 128], 0.0)
                r.V.mark(sem["v"], 1) if i2 != 0 or True else None
                vmark_own = sem["v"].n
                r.T.wait(sem["v"], vmark_own)
                last = (i2 == 3)
                r.T.matmul(PS[6][0:1, 0:512], c.ones_col[:, :], e_b[:, og % 8, :],
                           start=False, stop=last)
                r.T.matmul(PS[7][:, 0:512], vN[:, g * 4 + i2, 0:128],
                           e_b[:, og % 8, :], start=False, stop=last)
                r.T.matmul(PS[3][:, 0:512], vN[:, g * 4 + i2, 128:256],
                           e_b[:, og % 8, :], start=False, stop=last
                           ).then_inc(sem["t2"], 1)
                avm2[og] = sem["t2"].n
            r.V.wait(sem["t2"], avm2[67])
            r.V.tensor_copy(c.rows_bf[0:1, 0:512], PS[6][0:1, 0:512]
                            ).then_inc(sem["v2"], 1)
            r.T.wait(sem["v2"])
            r.T.matmul(PS[2][:, 0:512], c.ones_rb[:, :], c.rows_bf[0:1, 0:512],
                       start=True, stop=True).then_inc(sem["t"], 1)
            r.A.wait(sem["t"])
            r.A.activation(rbb[:, :], PS[2][:, 0:512], AF.Ln)
            r.A.activation(rbb[:, :], rbb[:, :], AF.Exp, scale=-1.0)
            r.A.mark(sem["a"], 1)
            r.V.wait(sem["a"])
            r.V.scalar_tensor_tensor(o_sb[:, 0, q0:q0 + 512], PS[7][:, 0:512], 1.0,
                                     rbb[:, :], op0=ALU.mult, op1=ALU.mult)
            r.V.scalar_tensor_tensor(o_sb[:, 1, q0:q0 + 512], PS[3][:, 0:512], 1.0,
                                     rbb[:, :], op0=ALU.mult, op1=ALU.mult)
            r.V.mark(sem["v2"], 1)
            r.T.wait(sem["v2"])

        for lt in range(4):
            for chn in range(2):
                pb = PS[lt % 2]
                for dct in range(2):
                    r.T.matmul(pb[:, 0:512], wc2lb[:, dct, lt * 128:(lt + 1) * 128],
                               o_sb[:, dct, chn * 512:(chn + 1) * 512],
                               start=(dct == 0), stop=(dct == 1))
                r.T.mark(sem["t"], 1)
                r.V.wait(sem["t"])
                r.V.tensor_copy(zT[:, lt, chn * 512:(chn + 1) * 512], pb[:, 0:512]
                                ).then_inc(sem["v"], 1)
                r.T.wait(sem["v"])
        r.T.wait(sem["onep"])
        for ot in range(16):
            for chn in range(2):
                pb = PS[(ot % 2) * 2 + chn]
                for lt in range(4):
                    r.T.matmul(pb[:, 0:512], wkvnb[:, lt, ot * 128:(ot + 1) * 128],
                               zT[:, lt, chn * 512:(chn + 1) * 512],
                               start=(lt == 0), stop=False)
                for lt in range(4):
                    r.T.matmul(pb[:, 0:512], worb[:, lt, ot * 128:(ot + 1) * 128],
                               zT[:, lt, chn * 512:(chn + 1) * 512],
                               start=False, stop=(lt == 3))
            r.T.mark(sem["t"], 1)
            r.V.wait(sem["t"])
            for chn in range(2):
                pb = PS[(ot % 2) * 2 + chn]
                r.V.scalar_tensor_tensor(
                    c.xT[:, ot, chn * 512:(chn + 1) * 512], pb[:, 0:512],
                    bor[:, ot:ot + 1], c.xT[:, ot, chn * 512:(chn + 1) * 512],
                    op0=ALU.add, op1=ALU.add)
            r.V.mark(sem["v"], 1)
            r.T.wait(sem["v"])
        r.replay(nc)


def phase_out(c):
    nc, sem = c.nc, c.sem
    r = Rec()
    if getattr(c, "out_streamed", False):
        r.S.wait(sem["dout"])
    else:
        r.S.dma_start(out=c.out_xT.ap().rearrange("(dt p) n -> p dt n", p=128),
                      in_=c.xT[:, :, :]).then_inc(sem["dout"], 16)
        r.S.wait(sem["dout"])
    r.replay(nc)


def _host_prep(I):
    f4 = np.float32
    x = np.ascontiguousarray(I["L0_attn_a_x"][0], dtype=f4)
    perm = np.concatenate([np.arange(0, HD, 2), np.arange(1, HD, 2)])
    Wq, Wk = I["L0_attn_a_Wq"], I["L0_attn_a_Wk"]
    wqT = np.ascontiguousarray(Wq.reshape(H, HD, D)[:, perm, :].reshape(D, D).T, f4)
    wkT = np.ascontiguousarray(Wk.reshape(H, HD, D)[:, perm, :].reshape(D, D).T, f4)
    wvT = np.ascontiguousarray(I["L0_attn_a_Wv"].T, f4)
    woT = np.ascontiguousarray(I["L0_attn_a_Wo"].T, f4)
    inv = (1.0 / ROPE ** (np.arange(0, HD, 2) / HD)).astype(np.float64)

    ident_bf = np.eye(128, dtype=BF)
    ident_f = np.eye(128, dtype=f4)
    ones_col = np.ones((128, 1), dtype=BF)
    ones_row = np.ones((1, 128), dtype=f4)
    p_ = np.arange(128)[:, None]
    i_ = np.arange(128)[None, :]
    tri_prev = (p_ > i_).astype(BF)
    tri_cur = (p_ <= i_).astype(BF)

    wupqT = np.ascontiguousarray(I["L0_ffn_w_up_q"].T.astype(np.int32))
    upscT = np.ascontiguousarray(np.repeat(I["L0_ffn_w_up_scale"], 64, axis=1).T.astype(f4))
    wdnqT = np.ascontiguousarray(I["L0_ffn_w_down_q"].T.astype(np.int32))
    dnscT = np.ascontiguousarray(np.repeat(I["L0_ffn_w_down_scale"], 64, axis=1).T.astype(f4))
    bup = np.ascontiguousarray(I["L0_ffn_b_up"].astype(f4).reshape(64, 128).T)
    bdn = np.ascontiguousarray(I["L0_ffn_b_down"].astype(f4).reshape(16, 128).T)

    wkvT = np.ascontiguousarray(I["L0_attn_b_W_to_kv"].T, f4)
    wkv_nat = np.ascontiguousarray(I["L0_attn_b_W_to_kv"], f4)
    wcT = np.ascontiguousarray(I["L0_attn_b_W_compress"].T, f4)
    wqcT = np.ascontiguousarray(I["L0_attn_b_W_q_comp"].T, f4)
    wkvcT = np.ascontiguousarray(I["L0_attn_b_W_kv_comp"].T, f4)
    wc2lT = np.ascontiguousarray(I["L0_attn_b_W_comp2lat"].T, f4)
    worT = np.ascontiguousarray(I["L0_attn_b_W_out_res"].T, f4)
    gb = np.zeros((128, 4), f4)
    gb[:, 0] = I["L0_attn_b_ln_gamma"][0:128]
    gb[:, 1] = I["L0_attn_b_ln_gamma"][128:256]
    gb[:, 2] = I["L0_attn_b_ln_beta"][0:128]
    gb[:, 3] = I["L0_attn_b_ln_beta"][128:256]
    bor = np.ascontiguousarray(I["L0_attn_b_b_out_res"].astype(f4).reshape(16, 128).T)
    invb = (1.0 / ROPE ** (np.arange(0, DC, 2) / DC)).astype(np.float64)

    in_maps = []
    for cc in range(NC):
        r0 = cc * SC
        xo = np.ascontiguousarray(x[r0:r0 + SC].T)
        xh = (np.zeros((D, WIN), f4) if cc == 0
              else np.ascontiguousarray(x[r0 - WIN:r0].T))
        pos = np.arange(r0 - WIN, r0 + SC, dtype=np.float64)
        th = pos[:, None] * inv[None, :]
        ta_cos = np.ascontiguousarray(
            np.concatenate([np.cos(th), np.cos(th)], 1).T.astype(f4))
        ta_sin = np.ascontiguousarray(
            np.concatenate([np.sin(th), -np.sin(th)], 1).T.astype(f4))
        za = np.zeros((128, 16), f4)
        if cc == 0:
            za[:, 0:4] = NEG
        posb = np.arange(r0, r0 + SC, dtype=np.float64)
        frb = posb[:, None] * invb[None, :]
        tb_cos = np.ascontiguousarray(
            np.concatenate([np.cos(frb), np.cos(frb)], 1).T.astype(f4))
        tb_sin = np.ascontiguousarray(
            np.concatenate([-np.sin(frb), np.sin(frb)], 1).T.astype(f4))
        zb = np.zeros((128, 128), f4)
        for g in range(2):
            d0 = 8 * cc + 4 * g
            zb[:, g * 64 + d0: g * 64 + 64] = NEG
        m = dict(
            xT_own=xo, xT_halo=xh,
            wqT=wqT, wkT=wkT, wvT=wvT, woT=woT,
            ta_cos=ta_cos, ta_sin=ta_sin, zadd_a_i=za,
            ident_bf_i=ident_bf, ident_f_i=ident_f,
            ones_col_i=ones_col, ones_row_i=ones_row,
            ones_rb_i=np.ones((1, 128), dtype=ml_dtypes.bfloat16),
            tri_prev_i=tri_prev, tri_cur_i=tri_cur,
            wupq_sh=np.ascontiguousarray(wupqT[cc * 256:(cc + 1) * 256]),
            upsc_sh=np.ascontiguousarray(upscT[cc * 256:(cc + 1) * 256]),
            wdnq_sh=np.ascontiguousarray(wdnqT[cc * 1024:(cc + 1) * 1024]),
            dnsc_sh=np.ascontiguousarray(dnscT[cc * 1024:(cc + 1) * 1024]),
            bup_i=bup, bdn_i=bdn,
            loraA1=np.ascontiguousarray(I["L0_ffn_w_up_lora_a"], f4),
            loraB1=np.ascontiguousarray(I["L0_ffn_w_up_lora_b"], f4),
            loraA2=np.ascontiguousarray(I["L0_ffn_w_down_lora_a"], f4),
            loraB2=np.ascontiguousarray(I["L0_ffn_w_down_lora_b"], f4),
            wkvT=wkvT, wkv_nat=wkv_nat, wcT=wcT, wqcT=wqcT, wkvcT=wkvcT,
            wc2lT=wc2lT, worT=worT, gb_i=gb, bor_i=bor,
            tb_cos=tb_cos, tb_sin=tb_sin, zadd_b_i=zb,
        )
        in_maps.append(m)
    return in_maps


def kernel(**inputs):
    in_maps = _host_prep(inputs)
    c = build(_CUT)
    res = run_bass_kernel_spmd(c.nc, in_maps, core_ids=list(range(NC)))
    shards = [res.results[i]["out_xT"] for i in range(NC)]
    out = np.concatenate([s.T for s in shards], axis=0)[None]
    return np.ascontiguousarray(out, dtype=np.float32)


if __name__ == "__main__":
    print("kernel module (see validate.py)")
